# version 23
# baseline (speedup 1.0000x reference)
import numpy as np
from contextlib import ExitStack
from collections import defaultdict

import concourse.bass as bass
import concourse.mybir as mybir
from concourse.bass_utils import run_bass_kernel_spmd

F32 = mybir.dt.float32
BF16 = mybir.dt.bfloat16
I32 = mybir.dt.int32
AF = mybir.ActivationFunctionType
ALU = mybir.AluOpType

D = 1024
NF = 8
H = 16
DH = 64
CK = 128
FH = 3072
NJ = 24
QSC = 1.0 / 127.0
EPS = 1e-5
QW = 256

ENGINES = ("pe", "act", "dve", "sp", "pool")


class Ev:
    __slots__ = ("sem", "val")

    def __init__(self, sem, val):
        self.sem = sem
        self.val = val


class FW:

    def __init__(self, nc, ctx):
        self.nc = nc
        self.q = {e: [] for e in ENGINES}
        self.semcnt = defaultdict(int)
        self.hiwater = defaultdict(int)
        self.sems = {}
        self.ctx = ctx

    def sem(self, name):
        if name not in self.sems:
            self.sems[name] = self.ctx.enter_context(self.nc.semaphore("s_" + name))
        return self.sems[name]

    def wait(self, engine, evs):
        if evs is None:
            return
        if isinstance(evs, Ev):
            evs = [evs]
        best = {}
        for ev in evs:
            if ev is None:
                continue
            if ev.val > best.get(ev.sem, 0):
                best[ev.sem] = ev.val
        for semname, val in sorted(best.items()):
            if self.hiwater[(engine, semname)] >= val:
                continue
            self.hiwater[(engine, semname)] = val
            sem = self.sem(semname)
            self.q[engine].append(("wait", sem, val))

    def emit(self, engine, fn, wait=None, inc=None):
        self.wait(engine, wait)
        ev = None
        if inc is not None:
            semname, n = inc if isinstance(inc, tuple) else (inc, 1)
            self.semcnt[semname] += n
            ev = Ev(semname, self.semcnt[semname])
            sem = self.sem(semname)
            self.q[engine].append(("op", fn, sem, n))
        else:
            self.q[engine].append(("op", fn, None, 0))
        return ev

    def dma(self, engine, fn, wait=None, sem="dma"):
        return self.emit(engine, fn, wait=wait, inc=(sem, 16))

    def finish(self, block):
        def mk_body(engine):
            def body(eng):
                for item in self.q[engine]:
                    if item[0] == "wait":
                        eng.wait_ge(item[1], item[2])
                    else:
                        ins = item[1](eng)
                        if item[2] is not None:
                            ins.then_inc(item[2], item[3])
            return body
        block.tensor(mk_body("pe"))
        block.scalar(mk_body("act"))
        block.vector(mk_body("dve"))
        block.sync(mk_body("sp"))
        block.gpsimd(mk_body("pool"))


def sblock(h):
    b = h // 2
    return (1, 0) if b == 7 else (0, 65 * b)


def build_program(T):
    nCH = T // CK
    nQ = T // QW
    nG = nCH // 4
    TH = T // 2
    n4 = max(TH // 512, 1)
    W4 = min(512, TH)
    n512 = T // 512

    nc = bass.Bass("TRN2", target_bir_lowering=False)
    x_d = nc.dram_tensor("x", [D, T], F32, kind="ExternalInput")
    wqt_d = nc.dram_tensor("wqt", [D, D], F32, kind="ExternalInput")
    wkt_d = nc.dram_tensor("wkt", [D, D], F32, kind="ExternalInput")
    wvt_d = nc.dram_tensor("wvt", [D, D], F32, kind="ExternalInput")
    wot_d = nc.dram_tensor("wot", [D, D], F32, kind="ExternalInput")
    bias_d = nc.dram_tensor("bias_pack", [128, 32], F32, kind="ExternalInput")
    bvrow_d = nc.dram_tensor("bv_row", [1, D], F32, kind="ExternalInput")
    wgt_d = nc.dram_tensor("wgt", [D, FH], I32, kind="ExternalInput")
    wut_d = nc.dram_tensor("wut", [D, FH], I32, kind="ExternalInput")
    wdt_d = nc.dram_tensor("wdt", [FH, D], I32, kind="ExternalInput")
    coef_d = nc.dram_tensor("coef", [128, 1], F32, kind="ExternalInput")
    out_d = nc.dram_tensor("out", [D, T], F32, kind="ExternalOutput")
    cc_in = nc.dram_tensor("cc_in", [128, 520], F32)
    h_dram = nc.dram_tensor("h_dram", [FH, T], BF16)
    cc_out = nc.dram_tensor("cc_out", [256, 520], F32)

    with ExitStack() as ctx:
        sb = lambda n, s, d: ctx.enter_context(nc.sbuf_tensor(n, s, d))
        fw = FW(nc, ctx)

        ident = sb("ident", [128, 128], BF16)
        ident32 = sb("ident32", [128, 128], F32)
        ones_c = sb("ones_c", [128, 1], BF16)
        onesr = sb("onesr", [1, 128], BF16)
        c1k = sb("c1k", [1, 128], BF16)
        ones1f = sb("ones1f", [1, 128], F32)
        bias_sb = sb("bias_sb", [128, 32], F32)
        bvb = sb("bvb", [128, D], BF16)
        coef_sb = sb("coef_sb", [128, 1], F32)
        xn = sb("xn", [128, NF * T], BF16)
        wgsb = [sb(f"wgsb{b}", [128, D], BF16) for b in range(2)]
        wusb = [sb(f"wusb{b}", [128, D], BF16) for b in range(2)]
        S_run = sb("S_run", [128, 520], F32)
        epsc = sb("epsc", [1, 1], F32)
        tblw = sb("tblw", [1, 1], F32)
        S_runb = sb("S_runb", [128, 520], BF16)

        block = ctx.enter_context(nc.Block())

        cBVR = ExitStack()
        bvr = cBVR.enter_context(nc.sbuf_tensor("bvr", [1, D], F32))
        cPSBV = ExitStack()
        ps_bv = cPSBV.enter_context(nc.psum_tensor("ps_bv", [128, 512], F32))

        e_bias = fw.dma("sp", lambda s: s.dma_start(out=bias_sb[:, :], in_=bias_d[:, :]), sem="ld_bias")
        e_coef = fw.dma("sp", lambda s: s.dma_start(out=coef_sb[:, :], in_=coef_d[:, :]), sem="ld_coef")
        e_ic0 = fw.emit("pool", lambda g: g.memset(ident[:, :], 1.0), inc="pool")
        e_id = fw.emit("pool", lambda g: g.affine_select(
            ident[:, :], ident[:, :], [[1, 128]], ALU.is_equal, 0.0,
            base=0, channel_multiplier=-1), wait=e_ic0, inc="pool")
        e_i320 = fw.emit("pool", lambda g: g.memset(ident32[:, :], 1.0), inc="pool")
        e_id32 = fw.emit("pool", lambda g: g.affine_select(
            ident32[:, :], ident32[:, :], [[1, 128]], ALU.is_equal, 0.0,
            base=0, channel_multiplier=-1), wait=e_i320, inc="pool")
        e_oc = fw.emit("dve", lambda v: v.memset(ones_c[:, :], 1.0), inc="dve")
        e_eps = fw.emit("dve", lambda v: v.memset(epsc[:, :], EPS), inc="dve")
        e_or = fw.emit("dve", lambda v: v.memset(onesr[:, :], 1.0), inc="dve")
        e_c1k = fw.emit("dve", lambda v: v.memset(c1k[:, :], 1.0 / D), inc="dve")
        e_o1f = fw.emit("dve", lambda v: v.memset(ones1f[:, :], 1.0), inc="dve")
        fw.emit("act", lambda sc: sc.activation(
            out=tblw[:, :], in_=epsc[0:1, 0:1], func=AF.Ln, scale=1.0),
            wait=[e_eps], inc="act")

        e_bvr = fw.dma("sp", lambda s: s.dma_start(out=bvr[:, :], in_=bvrow_d[:, :]), sem="ld_bvr")
        e_bvbc = []
        for half in range(2):
            e_mm = fw.emit("pe", lambda t, half=half: t.matmul(
                ps_bv[:, 0:512], ones1f[:, :], bvr[:, 512 * half:512 * half + 512],
                start=True, stop=True),
                wait=[e_o1f, e_bvr] + e_bvbc[-1:], inc="pe")
            e_ev = fw.emit("dve", lambda v, half=half: v.tensor_copy(
                bvb[:, 512 * half:512 * half + 512], ps_bv[:, 0:512]),
                wait=e_mm, inc="dve")
            e_bvbc.append(e_ev)
        cPSBV.close()
        cBVR.close()


        cS2 = ExitStack()
        sb2 = lambda n, s, d: cS2.enter_context(nc.sbuf_tensor(n, s, d))
        kfm = sb2("kfm", [128, NF * T], BF16)
        vtm = sb2("vtm", [128, nCH * 1040], BF16)
        qfm = sb2("qfm", [128, NF * T], BF16)
        wsb = [sb2(f"wsb{b}", [128, NF * D], BF16) for b in range(2)]
        ktm = [sb2(f"ktm{b}", [128, D], BF16) for b in range(2)]

        cS1 = ExitStack()
        sb1 = lambda n, s, d: cS1.enter_context(nc.sbuf_tensor(n, s, d))
        bfx = [sb1(f"bfx{b}", [128, NF * QW], BF16) for b in range(2)]
        sqr = [sb1(f"sqr{b}", [128, QW], BF16) for b in range(2)]
        rows = [sb1(f"rows{b}", [1, 4 * QW], F32) for b in range(2)]
        rowb = [sb1(f"rowb{b}", [1, 2 * QW], BF16) for b in range(2)]
        tmp = [sb1(f"tmp{b}", [128, QW], F32) for b in range(2)]
        phie = [sb1(f"phie{b}", [128, 512], BF16) for b in range(2)]
        phir = [sb1(f"phir{b}", [128, 512], BF16) for b in range(2)]
        S_in = sb1("S_in", [128, 520], F32)

        e_vone = fw.emit("dve", lambda v: v.memset(
            bass.AP(vtm, 64, [[nCH * 1040, 128], [1040, nCH], [65, H], [1, 1]]), 1.0),
            inc="dve")

        cPS2 = ExitStack()
        ps2f = lambda n, s, d: cPS2.enter_context(nc.psum_tensor(n, s, d))
        ps_st = ps2f("ps_st", [1, 512], F32)
        ps_mr = ps2f("ps_mr", [128, 512], F32)
        ps_p = [ps2f(f"ps_p{i}", [128, 512], F32) for i in range(2)]
        ps_S = ps2f("ps_S", [128, 512], F32)
        ps_S2 = ps2f("ps_S2", [128, 512], F32)
        ps_ktm = ps2f("ps_ktm", [128, 1024], BF16)

        def wload(dram, slot, semname, war=None):
            return fw.dma("pool", lambda g, slot=slot, dram=dram: g.dma_start(
                out=wsb[slot][:, :],
                in_=bass.AP(dram, 0, [[D, 128], [128 * D, NF], [1, D]])),
                wait=war, sem=semname)

        ld_ev = [None] * nQ
        ap_ev = [None] * nQ
        row_ev = [None] * nQ
        rowwar_ev = [None] * nQ
        bc_ev = [None] * nQ
        s2mm_ev = {}

        def emit_p1_quarter(qi):
            b = qi % 2
            cons = (ap_ev[qi - 2] + rowwar_ev[qi - 2]) if qi >= 2 else None
            ld_ev[qi] = fw.dma("pool", lambda g, b=b, qi=qi: g.dma_start(
                out=bfx[b][:, :],
                in_=bass.AP(x_d, QW * qi, [[T, 128], [128 * T, NF], [1, QW]])),
                wait=cons, sem=f"bfx{b}")
            e_s1 = None
            for j in range(NF):
                ws = [ld_ev[qi], e_oc] + e_bvbc[-1:]
                if qi >= 1:
                    ws += rowwar_ev[qi - 1]
                e_s1 = fw.emit("pe", lambda t, j=j, b=b: t.matmul(
                    ps_st[:, 0:QW], ones_c[:, :], bfx[b][:, QW * j:QW * (j + 1)],
                    start=(j == 0), stop=(j == NF - 1), skip_group_check=True),
                    wait=ws, inc="pe" if j == NF - 1 else None)
                sq_war = ([s2mm_ev[(qi, j - 2)]] if j >= 2 else
                          ([s2mm_ev[(qi - 1, NF - 2 + j)]] if qi >= 1 else []))
                e_sq = fw.emit("act", lambda sc, j=j, b=b: sc.activation(
                    out=sqr[j % 2][:, :], in_=bfx[b][:, QW * j:QW * (j + 1)],
                    func=AF.Square),
                    wait=[ld_ev[qi]] + sq_war, inc="act")
                s2mm_ev[(qi, j)] = fw.emit("pe", lambda t, j=j: t.matmul(
                    ps_st[:, QW:2 * QW], ones_c[:, :], sqr[j % 2][:, :],
                    start=False, stop=(j == NF - 1), skip_group_check=True),
                    wait=e_sq, inc="pe")
            e_s2 = s2mm_ev[(qi, NF - 1)]
            R = rows[b]
            e_r1 = fw.emit("dve", lambda v, R=R: v.tensor_copy(R[:, 0:QW], ps_st[:, 0:QW]),
                           wait=[e_s1, e_s2] + (rowwar_ev[qi - 2] if qi >= 2 else []), inc="dve")
            e_r2 = fw.emit("dve", lambda v, R=R: v.tensor_copy(R[:, QW:2 * QW], ps_st[:, QW:2 * QW]),
                           wait=[e_s2], inc="dve")
            e_a = fw.emit("dve", lambda v, R=R: v.tensor_mul(
                R[:, 2 * QW:3 * QW], R[:, 0:QW], R[:, 0:QW]), wait=[e_r1], inc="dve")
            e_b = fw.emit("dve", lambda v, R=R: v.scalar_tensor_tensor(
                R[:, 3 * QW:4 * QW], R[:, 2 * QW:3 * QW], -1.0 / D, R[:, QW:2 * QW],
                ALU.mult, ALU.add), wait=[e_a, e_r2], inc="dve")
            e_l = fw.emit("act", lambda sc, R=R: sc.activation(
                out=R[:, 2 * QW:3 * QW], in_=R[:, 3 * QW:4 * QW], func=AF.Ln,
                scale=1.0 / D, bias=epsc[0:1, 0:1]), wait=[e_b, e_eps], inc="act")
            e_rs = fw.emit("act", lambda sc, R=R: sc.activation(
                out=R[:, 3 * QW:4 * QW], in_=R[:, 2 * QW:3 * QW], func=AF.Exp,
                scale=-0.5), wait=e_l, inc="act")
            RB = rowb[b]
            e_m16 = fw.emit("dve", lambda v, R=R, RB=RB: v.tensor_copy(RB[:, 0:QW], R[:, 0:QW]),
                            wait=[e_r1] + ([bc_ev[qi - 2]] if qi >= 2 else []), inc="dve")
            e_rb16 = fw.emit("dve", lambda v, R=R, RB=RB: v.tensor_copy(
                RB[:, QW:2 * QW], R[:, 3 * QW:4 * QW]), wait=[e_rs], inc="dve")
            row_ev[qi] = e_rs
            rowwar_ev[qi] = [e_rs, e_m16, e_rb16]
            war = ap_ev[qi - 1] if qi >= 1 else []
            e_bm = fw.emit("pe", lambda t, RB=RB: t.matmul(
                ps_mr[:, 0:QW], c1k[:, :], RB[:, 0:QW],
                start=True, stop=True, skip_group_check=True),
                wait=[e_m16, e_c1k] + war, inc="pe")
            e_br = fw.emit("pe", lambda t, RB=RB: t.matmul(
                ps_mr[:, QW:2 * QW], onesr[:, :], RB[:, QW:2 * QW],
                start=False, stop=True, skip_group_check=True),
                wait=[e_rb16, e_or], inc="pe")
            evs = []
            for j in range(NF):
                e_sub = fw.emit("dve", lambda v, j=j, b=b: v.tensor_sub(
                    tmp[j % 2][:, :], bfx[b][:, QW * j:QW * (j + 1)], ps_mr[:, 0:QW]),
                    wait=[e_bm] + ([evs[j - 2]] if j >= 2 else []), inc="dve")
                e_mu = fw.emit("dve", lambda v, j=j, qi=qi: v.tensor_mul(
                    xn[:, j * T + QW * qi: j * T + QW * (qi + 1)],
                    tmp[j % 2][:, :], ps_mr[:, QW:2 * QW]),
                    wait=[e_sub, e_br], inc="dve")
                evs.append(e_mu)
            ap_ev[qi] = evs
            bc_ev[qi] = evs[-1]

        p_ev = [None, None]
        vch_ev = [None] * nCH
        vstate = {"last_vmm": None}
        kph_ev = {}
        trstate = {"ktm_war": [None, None], "ktm_bank": None, "last_Smm": None}

        def emit_v(i):
            ve = []
            for oh in range(2):
                sl = oh
                qi0 = (i * CK) // QW
                need = [ev for q in range(qi0, qi0 + max(CK // QW, 1)) for ev in ap_ev[q]]
                def vmm(t, i=i, oh=oh, sl=sl):
                    r = None
                    for kk in range(NF):
                        r = t.matmul(ps_p[sl][:, 0:512],
                                     xn[:, kk * T + CK * i: kk * T + CK * (i + 1)],
                                     wsb[0][:, D * kk + 512 * oh: D * kk + 512 * oh + 512],
                                     start=(kk == 0), stop=(kk == NF - 1))
                    return r
                vstate["last_vmm"] = fw.emit("pe", vmm,
                                             wait=[e_wv, p_ev[sl]] + need, inc="pe")
                e_evac = fw.emit("dve", lambda v, i=i, oh=oh, sl=sl: v.tensor_add(
                    bass.AP(vtm, i * 1040 + 65 * 8 * oh, [[nCH * 1040, 128], [65, 8], [1, 64]]),
                    ps_p[sl][:, 0:512], bvb[:, 512 * oh:512 * oh + 512]),
                    wait=[vstate["last_vmm"]] + e_bvbc[-1:], inc="dve")
                p_ev[sl] = e_evac
                ve.append(e_evac)
            vch_ev[i] = ve

        def emit_k(t4):
            for och in range(NF):
                sl = och % 2
                qi0 = (t4 * 512) // QW
                need = [ev for q in range(qi0, qi0 + 512 // QW) for ev in ap_ev[q]]
                def kmm(t, och=och, t4=t4, sl=sl):
                    r = None
                    for kk in range(NF):
                        r = t.matmul(ps_p[sl][:, 0:512],
                                     wsb[1][:, D * kk + 128 * och: D * kk + 128 * och + 128],
                                     xn[:, kk * T + 512 * t4: kk * T + 512 * (t4 + 1)],
                                     start=(kk == 0), stop=(kk == NF - 1))
                    return r
                e_mm = fw.emit("pe", kmm, wait=[e_wk, p_ev[sl]] + need, inc="pe")
                e_e = fw.emit("act", lambda sc, och=och, sl=sl: sc.activation(
                    out=phie[sl][:, :], in_=ps_p[sl][:, 0:512], func=AF.Exp,
                    bias=bias_sb[:, 8 + och:9 + och]),
                    wait=[e_mm, e_bias], inc="act")
                e_r = fw.emit("act", lambda sc, och=och, sl=sl: sc.activation(
                    out=phir[sl][:, :], in_=ps_p[sl][:, 0:512], func=AF.Relu,
                    bias=bias_sb[:, 8 + och:9 + och]),
                    wait=[e_mm], inc="act")
                e_phi = fw.emit("dve", lambda v, och=och, t4=t4, sl=sl: v.scalar_tensor_tensor(
                    kfm[:, och * T + 512 * t4: och * T + 512 * (t4 + 1)],
                    phie[sl][:, :], 1.0, phir[sl][:, :], ALU.min, ALU.add),
                    wait=[e_e, e_r], inc="dve")
                p_ev[sl] = e_phi
                kph_ev[(och, t4)] = e_phi

        def emit_trS(i):
            t4 = (i * CK) // 512
            kneed = [kph_ev[(och, t4)] for och in range(NF)]
            def ktr(t, i=i):
                r = None
                for och in range(NF):
                    r = t.transpose(ps_ktm[:, 128 * och:128 * (och + 1)],
                                    kfm[:, och * T + CK * i: och * T + CK * (i + 1)],
                                    ident[:, :])
                return r
            e_tr = fw.emit("pe", ktr,
                           wait=kneed + [e_id, trstate["ktm_war"][i % 2], trstate["ktm_bank"]],
                           inc="pe")
            e_kev = fw.emit("dve", lambda v, i=i: v.tensor_copy(ktm[i % 2][:, :], ps_ktm[:, :]),
                            wait=e_tr, inc="dve")
            trstate["ktm_war"][i % 2] = e_kev
            trstate["ktm_bank"] = e_kev
            for h in range(H):
                blk, off = sblock(h)
                pst = ps_S2 if blk else ps_S
                pb = (h % 2) * 64
                first = (i == 0) and (h in (0, 1, 14, 15))
                trstate["last_Smm"] = fw.emit("pe", lambda t, i=i, h=h, pst=pst, off=off, pb=pb, first=first: t.matmul(
                    pst[pb:pb + 64, off:off + 65],
                    ktm[i % 2][:, DH * h:DH * (h + 1)],
                    bass.AP(vtm, i * 1040 + 65 * h, [[nCH * 1040, 128], [1, 65]]),
                    start=first, stop=True, skip_group_check=True,
                    tile_position=(0, pb)),
                    wait=[e_kev, vch_ev[i][h // 8], e_vone],
                    inc="pe" if (i == nCH - 1 and h == H - 1) else None)

        emit_p1_quarter(0)
        emit_p1_quarter(1)
        e_wv = wload(wvt_d, 0, "ld_wv")
        e_wk = wload(wkt_d, 1, "ld_wk")
        for qp in range(nQ // 2):
            if qp + 1 < nQ // 2:
                emit_p1_quarter(2 * qp + 2)
                emit_p1_quarter(2 * qp + 3)
            for i in range(4 * qp, min(4 * qp + 4, nCH)):
                emit_v(i)
            emit_k(qp)
            for i in range(4 * qp, min(4 * qp + 4, nCH)):
                emit_trS(i)
        p1_psum_done = ap_ev[nQ - 1] + rowwar_ev[nQ - 1] + \
            (rowwar_ev[nQ - 2] if nQ >= 2 else []) + e_bvbc[-1:]
        last_vmm = vstate["last_vmm"]
        ktm_war = trstate["ktm_war"]
        last_Smm = trstate["last_Smm"]

        e_wq = wload(wqt_d, 0, "ld_wq", war=[last_vmm])
        e_sl1 = fw.emit("dve", lambda v: v.tensor_copy(S_run[:, 0:455], ps_S[:, 0:455]),
                        wait=last_Smm, inc="dve")
        e_sl2 = fw.emit("dve", lambda v: v.tensor_copy(S_run[:, 455:520], ps_S2[:, 0:65]),
                        wait=last_Smm, inc="dve")
        e_ccs = fw.dma("sp", lambda s: s.dma_start(out=cc_in[:, :], in_=S_run[:, :]),
                       wait=[e_sl1, e_sl2], sem="st_cc")
        e_cc = fw.emit("pool", lambda g: g.collective_compute(
            "AllGather", ALU.bypass,
            replica_groups=[[0, 1], [2, 3], [4, 5], [6, 7]],
            ins=[cc_in.ap().opt()], outs=[cc_out.ap().opt()]),
            wait=e_ccs, inc="cc")

        for och in range(NF):
            for t4 in range(n512):
                sl = (och * n512 + t4) % 2
                def qmm(t, och=och, t4=t4, sl=sl):
                    r = None
                    for kk in range(NF):
                        r = t.matmul(ps_p[sl][:, 0:512],
                                     wsb[0][:, D * kk + 128 * och: D * kk + 128 * och + 128],
                                     xn[:, kk * T + 512 * t4: kk * T + 512 * (t4 + 1)],
                                     start=(kk == 0), stop=(kk == NF - 1))
                    return r
                e_mm = fw.emit("pe", qmm, wait=[e_wq, p_ev[sl]], inc="pe")
                e_e = fw.emit("act", lambda sc, och=och, sl=sl: sc.activation(
                    out=phie[sl][:, :], in_=ps_p[sl][:, 0:512], func=AF.Exp,
                    bias=bias_sb[:, och:och + 1]),
                    wait=[e_mm], inc="act")
                e_r = fw.emit("act", lambda sc, och=och, sl=sl: sc.activation(
                    out=phir[sl][:, :], in_=ps_p[sl][:, 0:512], func=AF.Relu,
                    bias=bias_sb[:, och:och + 1]),
                    wait=[e_mm], inc="act")
                e_phi = fw.emit("dve", lambda v, och=och, t4=t4, sl=sl: v.scalar_tensor_tensor(
                    qfm[:, och * T + 512 * t4: och * T + 512 * (t4 + 1)],
                    phie[sl][:, :], 1.0, phir[sl][:, :], ALU.min, ALU.add),
                    wait=[e_e, e_r], inc="dve")
                p_ev[sl] = e_phi
                kph_ev[("q", och, t4)] = e_phi
        e_wo = wload(wot_d, 1, "ld_wo", war=[kph_ev[(NF - 1, n512 - 1)]])
        q_done = [kph_ev[("q", och, t4)] for och in range(NF) for t4 in range(n512)]
        e_sin = fw.dma("sp", lambda s: s.dma_start(out=S_in[:, :], in_=cc_out[0:128, :]),
                       wait=e_cc, sem="ld_sin")
        e_sr = fw.emit("dve", lambda v: v.tensor_single_scalar(
            S_run[:, :], S_in[:, :], coef_sb[:, 0:1], ALU.mult),
            wait=[e_sin, e_coef, e_ccs], inc="dve")
        e_srb = fw.emit("dve", lambda v: v.tensor_copy(S_runb[:, :], S_run[:, :]),
                        wait=e_sr, inc="dve")
        wg_ev = [None, None]
        wu_ev = [None, None]
        for j in range(2):
            wg_ev[j] = fw.dma("pool", lambda g, j=j: g.dma_start(
                out=wgsb[j][:, :],
                in_=bass.AP(wgt_d, 128 * j, [[FH, 128], [128 * FH, NF], [1, 128]])),
                sem=f"ld_wg{j}")
            wu_ev[j] = fw.dma("pool", lambda g, j=j: g.dma_start(
                out=wusb[j][:, :],
                in_=bass.AP(wut_d, 128 * j, [[FH, 128], [128 * FH, NF], [1, 128]])),
                sem=f"ld_wu{j}")
        cPS2.close()
        cS1.close()

        cS4 = ExitStack()
        sb4 = lambda n, s, d: cS4.enter_context(nc.sbuf_tensor(n, s, d))
        cPS4 = ExitStack()
        ps4f = lambda n, s, d: cPS4.enter_context(nc.psum_tensor(n, s, d))
        sTs = sb4("sTs", [128, 512], BF16)
        mask2 = sb4("mask2", [128, 256], BF16)
        A_tm = [sb4(f"A_tm{b}", [128, D], F32) for b in range(2)]
        A_fm = sb4("A_fm", [128, NF * 512], BF16)
        rcp = sb4("rcp", [128, 4], F32)
        ost = [sb4(f"ost{b}", [128, 512], F32) for b in range(2)]
        ps_sT = ps4f("ps_sT", [128, 1024], F32)
        ps_nm = ps4f("ps_nm", [128, 1024], F32)
        ps_fg = ps4f("ps_fg", [128, 512], F32)
        ps_fu = ps4f("ps_fu", [128, 512], F32)
        ps_kt4 = ps4f("ps_kt4", [128, 1024], BF16)
        ps_w = ps4f("ps_w", [128, 512], F32)
        WH = min(512, T)
        hbuf = [sb4(f"hbuf{b}", [128, WH], BF16) for b in range(2)]
        sgf = sb4("sgf", [128, WH], BF16)

        e_mk0 = fw.emit("pool", lambda g: g.memset(mask2[:, :], 1.0), inc="pool")
        e_mk = fw.emit("pool", lambda g: g.affine_select(
            mask2[:, :], mask2[:, :], [[0, 2], [1, 128]], ALU.is_ge, 0.0,
            base=0, channel_multiplier=-1), wait=e_mk0, inc="pool")

        scope_bar = [e_sl1, e_sl2, e_srb] + p1_psum_done + q_done + [p_ev[0], p_ev[1]]

        THF = T
        n4h = max(THF // 512, 1)

        class GUFiller:
            def __init__(self):
                self.seq = []
                for j in range(NJ):
                    for t4 in range(n4h):
                        self.seq.append((j, "g", t4))
                        self.seq.append((j, "u", t4))
                self.gi = 0
                self.kk = 0
                self.g_war = e_sl1
                self.u_war = e_sl2
                self.sg_war = None
                self.h_war = [None, None]
                self.gu_done = {}
                self.wev = {0: wg_ev[0], 1: wg_ev[1]}
                self.wuev = {0: wu_ev[0], 1: wu_ev[1]}
                self.hst = []
                self.last_mm = None
                self.first_bar = list(q_done) + p1_psum_done + [e_srb]
                self.nsg = 0
                self.nh = 0

            def done(self):
                return self.gi >= len(self.seq)

            def step(self, n):
                for _ in range(n):
                    if self.done():
                        return
                    j, kind, t4 = self.seq[self.gi]
                    b = j % 2
                    kk = self.kk
                    ws = None
                    if kk == 0:
                        ws = [self.wev[b] if kind == "g" else self.wuev[b],
                              self.g_war if kind == "g" else self.u_war]
                    wv = wgsb[b] if kind == "g" else wusb[b]
                    ps = ps_fg if kind == "g" else ps_fu
                    ev = fw.emit("pe", lambda t, kk=kk, t4=t4, wv=wv, ps=ps: t.matmul(
                        ps[:, 0:WH], wv[:, 128 * kk:128 * (kk + 1)],
                        xn[:, kk * T + WH * t4: kk * T + WH * (t4 + 1)],
                        start=(kk == 0), stop=(kk == NF - 1)),
                        wait=ws, inc="pe" if kk == NF - 1 else None)
                    self.kk += 1
                    if self.kk == NF:
                        self.kk = 0
                        self.gi += 1
                        if kind == "g":
                            e_sg = fw.emit("act", lambda sc: sc.activation(
                                out=sgf[:, :], in_=ps_fg[:, 0:WH], func=AF.Silu,
                                scale=QSC),
                                wait=[ev, self.sg_war] +
                                     (self.first_bar if self.nsg == 0 else []),
                                inc="act")
                            self.nsg += 1
                            self.g_war = e_sg
                            self.pend_sg = e_sg
                        else:
                            hsl = (j * n4h + t4) % 2
                            e_h = fw.emit("dve", lambda v, hsl=hsl: v.tensor_mul(
                                hbuf[hsl][:, :], sgf[:, :], ps_fu[:, 0:WH]),
                                wait=[ev, self.pend_sg, self.h_war[hsl]] +
                                     (self.first_bar if self.nh < 2 else []), inc="dve")
                            self.nh += 1
                            self.u_war = e_h
                            self.sg_war = e_h
                            self.gu_done[j] = e_h
                            e_st = fw.dma("sp", lambda s, j=j, t4=t4, hsl=hsl: s.dma_start(
                                out=h_dram[128 * j:128 * (j + 1), WH * t4:WH * (t4 + 1)],
                                in_=hbuf[hsl][:, :]),
                                wait=e_h, sem=f"hst{hsl}")
                            self.h_war[hsl] = e_st
                            self.hst.append(e_st)
                            if t4 == n4h - 1 and j + 2 < NJ:
                                jn = j + 2
                                self.wev[jn % 2] = fw.dma("pool", lambda g, jn=jn: g.dma_start(
                                    out=wgsb[jn % 2][:, :],
                                    in_=bass.AP(wgt_d, 128 * jn, [[FH, 128], [128 * FH, NF], [1, 128]])),
                                    wait=[self.gu_done[jn - 2]], sem=f"ld_wg{jn % 2}")
                                self.wuev[jn % 2] = fw.dma("pool", lambda g, jn=jn: g.dma_start(
                                    out=wusb[jn % 2][:, :],
                                    in_=bass.AP(wut_d, 128 * jn, [[FH, 128], [128 * FH, NF], [1, 128]])),
                                    sem=f"ld_wu{jn % 2}")

        guf = GUFiller()
        guf.step(96)
        sd_red = None
        sd_wr = None
        kt4_ev = None
        psw_ev = None
        wo_nm_ev = None
        afm_ev = {}
        msk_ev = {}
        rc_ev = {}
        dv_ev = {}
        m3e = {}
        ost_war = [None, None]
        upd_evs = [e_srb]
        for i in range(nCH):
            gslot = i % 4
            if i > 0:
                e_u1 = fw.emit("dve", lambda v: v.tensor_add(
                    S_run[:, 0:455], S_run[:, 0:455], ps_sT[:, 0:455]),
                    wait=[sd_wr], inc="dve")
                e_u2 = fw.emit("dve", lambda v: v.tensor_add(
                    S_run[:, 455:520], S_run[:, 455:520], ps_sT[:, 512:577]),
                    wait=[sd_wr], inc="dve")
                e_ub = fw.emit("dve", lambda v: v.tensor_copy(S_runb[:, :], S_run[:, :]),
                               wait=[e_u1, e_u2], inc="dve")
                sd_red = e_ub
                upd_evs = [e_ub]
            def ktr4(t, i=i):
                r = None
                for och in range(NF):
                    r = t.transpose(ps_kt4[:, 128 * och:128 * (och + 1)],
                                    kfm[:, och * T + CK * i: och * T + CK * (i + 1)],
                                    ident[:, :])
                return r
            e_tr4 = fw.emit("pe", ktr4,
                            wait=scope_bar + [ktm_war[i % 2], kt4_ev], inc="pe")
            e_k4 = fw.emit("dve", lambda v, i=i: v.tensor_copy(ktm[i % 2][:, :], ps_kt4[:, :]),
                           wait=e_tr4, inc="dve")
            kt4_ev = e_k4
            ktm_war[i % 2] = e_k4
            for p in range(H // 2):
                gp = i * (H // 2) + p
                m1e = None
                for hh in range(2):
                    h = 2 * p + hh
                    pb = (h % 2) * 64
                    m1e = fw.emit("pe", lambda t, i=i, h=h, pb=pb, hh=hh: t.matmul(
                        ps_sT[:, 512 * hh:512 * hh + 128],
                        kfm[pb:pb + 64, (h // 2) * T + CK * i: (h // 2) * T + CK * (i + 1)],
                        qfm[pb:pb + 64, (h // 2) * T + CK * i: (h // 2) * T + CK * (i + 1)],
                        start=True, stop=True, skip_group_check=True),
                        wait=scope_bar + [msk_ev.get(gp - 1)] + (upd_evs if p == 0 else []),
                        inc="pe" if hh == 1 else None)
                guf.step(9)
                e_m = fw.emit("dve", lambda v, p=p, gp=gp: v.tensor_mul(
                    sTs[:, 256 * (gp % 2):256 * (gp % 2) + 256],
                    bass.AP(ps_sT, 0, [[1024, 128], [512, 2], [1, 128]]),
                    mask2[:, :]),
                    wait=[m1e, e_mk, m3e.get(gp - 2)], inc="dve")
                msk_ev[gp] = e_m
                for hh in range(2):
                    h = 2 * p + hh
                    pb = (h % 2) * 64
                    fw.emit("pe", lambda t, i=i, h=h, hh=hh, gp=gp: t.matmul(
                        ps_nm[:, 512 * hh:512 * hh + 65],
                        sTs[:, 256 * (gp % 2) + 128 * hh:256 * (gp % 2) + 128 * hh + 128],
                        bass.AP(vtm, i * 1040 + 65 * h, [[nCH * 1040, 128], [1, 65]]),
                        start=True, stop=False, skip_group_check=True),
                        wait=[e_m, rc_ev.get(gp - 1), dv_ev.get(2 * (gp - 1) + hh),
                              wo_nm_ev if hh == 0 else None])
                    ev3 = fw.emit("pe", lambda t, h=h, i=i, pb=pb, hh=hh: t.matmul(
                        ps_nm[:, 512 * hh:512 * hh + 65],
                        qfm[pb:pb + 64, (h // 2) * T + CK * i: (h // 2) * T + CK * (i + 1)],
                        S_runb[pb:pb + 64, 65 * (h // 2):65 * (h // 2) + 65],
                        start=False, stop=True, skip_group_check=True),
                        wait=upd_evs, inc="pe" if hh == 1 else None)
                m3e[gp] = ev3
                e_rc = fw.emit("dve", lambda v, gp=gp: v.reciprocal(
                    rcp[:, 2 * (gp % 2):2 * (gp % 2) + 2],
                    bass.AP(ps_nm, 64, [[1024, 128], [512, 2], [1, 1]])),
                    wait=[ev3, dv_ev.get(2 * (gp - 2) + 1)], inc="dve")
                rc_ev[gp] = e_rc
                for hh in range(2):
                    h = 2 * p + hh
                    dv_ev[2 * gp + hh] = fw.emit("act", lambda sc, i=i, h=h, hh=hh, gp=gp: sc.activation(
                        out=A_tm[i % 2][:, DH * h:DH * (h + 1)],
                        in_=ps_nm[:, 512 * hh:512 * hh + 64],
                        func=AF.Identity, scale=rcp[:, 2 * (gp % 2) + hh:2 * (gp % 2) + hh + 1]),
                        wait=[e_rc, afm_ev.get((i - 2, 1))], inc="act")
            last_msk = msk_ev[i * (H // 2) + (H // 2) - 1]
            for h in range(H):
                blk, off = sblock(h)
                off = 512 if blk else off
                pb = (h % 2) * 64
                e_sd = fw.emit("pe", lambda t, i=i, h=h, off=off, pb=pb: t.matmul(
                    ps_sT[pb:pb + 64, off:off + 65],
                    ktm[i % 2][:, DH * h:DH * (h + 1)],
                    bass.AP(vtm, i * 1040 + 65 * h, [[nCH * 1040, 128], [1, 65]]),
                    start=True, stop=True, skip_group_check=True,
                    tile_position=(0, pb)),
                    wait=[kt4_ev, sd_red, last_msk], inc="pe" if h == H - 1 else None)
                if h == H - 1:
                    sd_wr = e_sd
            guf.step(2)
            chunk_divs = [dv_ev[2 * (i * (H // 2) + (H // 2) - 1) + 1]]
            for w in range(2):
                def atr(t, i=i, w=w):
                    r = None
                    for o in range(4):
                        och = 4 * w + o
                        r = t.transpose(ps_w[:, 128 * o:128 * (o + 1)],
                                        A_tm[i % 2][:, 128 * och:128 * (och + 1)],
                                        ident32[:, :])
                    return r
                e_at = fw.emit("pe", atr,
                               wait=chunk_divs + [psw_ev, wo_nm_ev, e_id32], inc="pe")
                e_ae = fw.emit("dve", lambda v, i=i, w=w, gslot=gslot: v.tensor_copy(
                    bass.AP(A_fm, 512 * 4 * w + 128 * gslot,
                            [[NF * 512, 128], [512, 4], [1, 128]]),
                    ps_w[:, 0:512]),
                    wait=e_at, inc="dve")
                psw_ev = e_ae
                afm_ev[(i, w)] = e_ae
            if gslot == 3:
                g4 = i // 4
                aneed = [afm_ev[(4 * g4 + c, w)] for c in range(4) for w in range(2)]
                last_gp = i * (H // 2) + (H // 2) - 1
                for och in range(NF):
                    if och % 2 == 0:
                        wob = ps_w[:, 0:512]
                        war = [psw_ev]
                    else:
                        wob = ps_nm[:, 0:512]
                        war = [rc_ev[last_gp], dv_ev[2 * last_gp], wo_nm_ev]
                    def womm(t, och=och, wob=wob):
                        r = None
                        for kk in range(NF):
                            r = t.matmul(wob,
                                         wsb[1][:, D * kk + 128 * och: D * kk + 128 * och + 128],
                                         A_fm[:, kk * 512:(kk + 1) * 512],
                                         start=(kk == 0), stop=(kk == NF - 1),
                                         skip_group_check=True)
                        return r
                    e_wm = fw.emit("pe", womm, wait=aneed + [e_wo] + war, inc="pe")
                    e_oev = fw.emit("dve", lambda v, och=och, wob=wob: v.tensor_scalar(
                        ost[och % 2][:, :], wob, bias_sb[:, 24 + och:25 + och],
                        None, ALU.add),
                        wait=[e_wm, e_bias, ost_war[och % 2]], inc="dve")
                    if och % 2 == 0:
                        psw_ev = e_oev
                    else:
                        wo_nm_ev = e_oev
                    ost_war[och % 2] = fw.dma("sp", lambda s, och=och, g4=g4: s.dma_start(
                        out=out_d[128 * och:128 * (och + 1), 512 * g4:512 * (g4 + 1)],
                        in_=ost[och % 2][:, :]),
                        wait=e_oev, sem=f"st_attn{och % 2}")
        n_attn_st = NF * nG
        last_gp_all = nCH * (H // 2) - 1
        p4_scan = [wo_nm_ev, psw_ev, kt4_ev, upd_evs[-1], sd_wr,
                   msk_ev[last_gp_all], rc_ev[last_gp_all],
                   dv_ev[2 * last_gp_all], dv_ev[2 * last_gp_all + 1]]
        cPS4.close()
        cS4.close()
        cS2.close()

        cS3 = ExitStack()
        sb3 = lambda n, s, d: cS3.enter_context(nc.sbuf_tensor(n, s, d))
        cPS3 = ExitStack()
        ps3f = lambda n, s, d: cPS3.enter_context(nc.psum_tensor(n, s, d))
        hfull = sb3("hfull", [128, NJ * T], BF16)
        wdsb = sb3("wdsb", [128, NJ * D], BF16)
        x3 = [sb3(f"x3{b}", [128, W4], F32) for b in range(2)]
        ost3 = [sb3(f"ost3{b}", [128, W4], F32) for b in range(2)]
        ps_d = [ps3f(f"ps_d{b}", [128, 512], F32) for b in range(2)]

        wd_parts = []
        h_parts = []

        def emit_wd():
            p = len(wd_parts)
            wd_parts.append(fw.dma("pool", lambda g, p=p: g.dma_start(
                out=wdsb[:, D * 3 * p:D * 3 * (p + 1)],
                in_=bass.AP(wdt_d, 128 * 1024 * 3 * p,
                            [[1024, 128], [128 * 1024, 3], [1, 1024]])),
                wait=p4_scan if p == 0 else None, sem="ld_wd"))

        def emit_h():
            p = len(h_parts)
            need = (4 * p + 4) * n4h
            hw = [guf.hst[need - 1], guf.hst[need - 2]]
            h_parts.append(fw.dma("act", lambda s, p=p: s.dma_start(
                out=hfull[:, 4 * T * p:4 * T * (p + 1)],
                in_=bass.AP(h_dram, 128 * T * 4 * p,
                            [[T, 128], [128 * T, 4], [1, T]])),
                wait=(p4_scan if p == 0 else []) + hw, sem="ld_h"))

        emit_wd()
        emit_h()
        while not guf.done():
            guf.step(16)
            if len(wd_parts) < 8:
                emit_wd()
            if (len(h_parts) < 6 and
                    (4 * len(h_parts) + 4) * n4h + 1 <= len(guf.hst)):
                emit_h()
        while len(wd_parts) < 8:
            emit_wd()
        while len(h_parts) < 6:
            emit_h()
        e_wd = wd_parts[-1]
        e_hload = h_parts[-1]

        p4_psum = p4_scan + [guf.g_war, guf.u_war]
        p4_done = list(p4_psum)
        st_attn_total = [Ev("st_attn0", fw.semcnt["st_attn0"]),
                         Ev("st_attn1", fw.semcnt["st_attn1"]),
                         Ev("hst0", fw.semcnt["hst0"]),
                         Ev("hst1", fw.semcnt["hst1"])]
        p4_done = p4_done + st_attn_total
        e_acc_last = None
        d_ev = [None, None]
        o3_ev = [None, None]

        def emit_down(hf, hprereq, hb):
            nonlocal e_acc_last
            for t4 in range(n4):
                for kk in range(NF):
                    sl = kk % 2
                    col = hf * TH + W4 * t4
                    e_x3 = fw.dma("sp", lambda s, kk=kk, sl=sl, col=col: s.dma_start(
                        out=x3[sl][:, :],
                        in_=x_d[128 * kk:128 * (kk + 1), col:col + W4]),
                        wait=[o3_ev[sl]] if o3_ev[sl] else p4_done, sem=f"x3_{sl}")
                    def dmm(t, kk=kk, sl=sl, col=col, hb=hb):
                        r = None
                        for j in range(NJ):
                            r = t.matmul(ps_d[sl][:, 0:W4],
                                         wdsb[:, D * j + 128 * kk: D * j + 128 * kk + 128],
                                         hb[:, j * T + col: j * T + col + W4],
                                         start=(j == 0), stop=(j == NJ - 1))
                        return r
                    e_dm = fw.emit("pe", dmm, wait=hprereq + [e_wd, d_ev[sl]], inc="pe")
                    e_o3 = fw.emit("dve", lambda v, kk=kk, sl=sl: v.scalar_tensor_tensor(
                        ost3[sl][:, :], ps_d[sl][:, 0:W4], QSC * QSC, x3[sl][:, :],
                        ALU.mult, ALU.add),
                        wait=[e_dm, e_x3, o3_ev[sl]], inc="dve")
                    d_ev[sl] = e_o3
                    e_st = fw.dma("pool", lambda g, kk=kk, sl=sl, col=col: g.dma_start(
                        out=out_d[128 * kk:128 * (kk + 1), col:col + W4],
                        in_=ost3[sl][:, :], accum_op=ALU.add),
                        wait=[e_o3] + st_attn_total, sem=f"st_out{sl}")
                    o3_ev[sl] = e_st
                    e_acc_last = e_st

        emit_down(0, [e_hload, e_wd] + p4_done, hfull)
        emit_down(1, [], hfull)
        cPS3.close()
        cS3.close()

        out_final = [Ev("st_out0", fw.semcnt["st_out0"]), Ev("st_out1", fw.semcnt["st_out1"])]
        fw.emit("pool", lambda g: g.nop(), wait=out_final)
        fw.emit("sp", lambda s: s.nop(), wait=st_attn_total + out_final)
        fw.finish(block)
    return nc


_NC_CACHE = {}


def _get_nc(T):
    if T not in _NC_CACHE:
        _NC_CACHE[T] = build_program(T)
    return _NC_CACHE[T]


def kernel(L0_attn_x, L0_attn_Wq, L0_attn_bq, L0_attn_Wk, L0_attn_bk,
           L0_attn_Wv, L0_attn_bv, L0_attn_Wo, L0_attn_bo,
           L0_ffn_w_gate, L0_ffn_w_up, L0_ffn_w_down):
    x = np.asarray(L0_attn_x)
    B, Sq, _ = x.shape
    T = (B * Sq) // 8
    xT = np.ascontiguousarray(x.transpose(0, 2, 1))

    def wT(w):
        return np.ascontiguousarray(np.asarray(w).T)

    wqt, wkt, wvt, wot = wT(L0_attn_Wq), wT(L0_attn_Wk), wT(L0_attn_Wv), wT(L0_attn_Wo)
    wgt = np.ascontiguousarray(np.asarray(L0_ffn_w_gate).T)
    wut = np.ascontiguousarray(np.asarray(L0_ffn_w_up).T)
    wdt = np.ascontiguousarray(np.asarray(L0_ffn_w_down).T)

    def pack(bv):
        return np.ascontiguousarray(np.asarray(bv, np.float32).reshape(8, 128).T)

    bias_pack = np.ascontiguousarray(np.concatenate(
        [pack(L0_attn_bq), pack(L0_attn_bk), pack(L0_attn_bv), pack(L0_attn_bo)], axis=1))
    bv_row = np.ascontiguousarray(np.asarray(L0_attn_bv, np.float32)[None, :])

    in_maps = []
    for c in range(8):
        b, g = c // 2, c % 2
        in_maps.append({
            "x": np.ascontiguousarray(xT[b][:, g * T:(g + 1) * T]),
            "wqt": wqt, "wkt": wkt, "wvt": wvt, "wot": wot,
            "bias_pack": bias_pack, "bv_row": bv_row,
            "wgt": wgt, "wut": wut, "wdt": wdt,
            "coef": np.full((128, 1), float(g), np.float32),
        })
    nc = _get_nc(T)
    res = run_bass_kernel_spmd(nc, in_maps, core_ids=list(range(8)))
    full = np.empty((B, D, Sq), np.float32)
    for c in range(8):
        b, g = c // 2, c % 2
        full[b][:, g * T:(g + 1) * T] = res.results[c]["out"]
    return np.ascontiguousarray(full.transpose(0, 2, 1))


# version 24
# speedup vs baseline: 1.0094x; 1.0094x over previous
import numpy as np
from contextlib import ExitStack
from collections import defaultdict

import concourse.bass as bass
import concourse.mybir as mybir
from concourse.bass_utils import run_bass_kernel_spmd

F32 = mybir.dt.float32
BF16 = mybir.dt.bfloat16
I32 = mybir.dt.int32
AF = mybir.ActivationFunctionType
ALU = mybir.AluOpType

D = 1024
NF = 8
H = 16
DH = 64
CK = 128
FH = 3072
NJ = 24
QSC = 1.0 / 127.0
EPS = 1e-5
QW = 256

ENGINES = ("pe", "act", "dve", "sp", "pool")


class Ev:
    __slots__ = ("sem", "val")

    def __init__(self, sem, val):
        self.sem = sem
        self.val = val


class FW:

    def __init__(self, nc, ctx):
        self.nc = nc
        self.q = {e: [] for e in ENGINES}
        self.semcnt = defaultdict(int)
        self.hiwater = defaultdict(int)
        self.sems = {}
        self.ctx = ctx

    def sem(self, name):
        if name not in self.sems:
            self.sems[name] = self.ctx.enter_context(self.nc.semaphore("s_" + name))
        return self.sems[name]

    def wait(self, engine, evs):
        if evs is None:
            return
        if isinstance(evs, Ev):
            evs = [evs]
        best = {}
        for ev in evs:
            if ev is None:
                continue
            if ev.val > best.get(ev.sem, 0):
                best[ev.sem] = ev.val
        for semname, val in sorted(best.items()):
            if self.hiwater[(engine, semname)] >= val:
                continue
            self.hiwater[(engine, semname)] = val
            sem = self.sem(semname)
            self.q[engine].append(("wait", sem, val))

    def emit(self, engine, fn, wait=None, inc=None):
        self.wait(engine, wait)
        ev = None
        if inc is not None:
            semname, n = inc if isinstance(inc, tuple) else (inc, 1)
            self.semcnt[semname] += n
            ev = Ev(semname, self.semcnt[semname])
            sem = self.sem(semname)
            self.q[engine].append(("op", fn, sem, n))
        else:
            self.q[engine].append(("op", fn, None, 0))
        return ev

    def dma(self, engine, fn, wait=None, sem="dma"):
        return self.emit(engine, fn, wait=wait, inc=(sem, 16))

    def finish(self, block):
        def mk_body(engine):
            def body(eng):
                for item in self.q[engine]:
                    if item[0] == "wait":
                        eng.wait_ge(item[1], item[2])
                    else:
                        ins = item[1](eng)
                        if item[2] is not None:
                            ins.then_inc(item[2], item[3])
            return body
        block.tensor(mk_body("pe"))
        block.scalar(mk_body("act"))
        block.vector(mk_body("dve"))
        block.sync(mk_body("sp"))
        block.gpsimd(mk_body("pool"))


def sblock(h):
    b = h // 2
    return (1, 0) if b == 7 else (0, 65 * b)


def build_program(T):
    nCH = T // CK
    nQ = T // QW
    nG = nCH // 4
    TH = T // 2
    n4 = max(TH // 512, 1)
    W4 = min(512, TH)
    n512 = T // 512

    nc = bass.Bass("TRN2", target_bir_lowering=False)
    x_d = nc.dram_tensor("x", [D, T], F32, kind="ExternalInput")
    wqt_d = nc.dram_tensor("wqt", [D, D], F32, kind="ExternalInput")
    wkt_d = nc.dram_tensor("wkt", [D, D], F32, kind="ExternalInput")
    wvt_d = nc.dram_tensor("wvt", [D, D], F32, kind="ExternalInput")
    wot_d = nc.dram_tensor("wot", [D, D], F32, kind="ExternalInput")
    bias_d = nc.dram_tensor("bias_pack", [128, 32], F32, kind="ExternalInput")
    bvrow_d = nc.dram_tensor("bv_row", [1, D], F32, kind="ExternalInput")
    wgt_d = nc.dram_tensor("wgt", [D, FH], I32, kind="ExternalInput")
    wut_d = nc.dram_tensor("wut", [D, FH], I32, kind="ExternalInput")
    wdt_d = nc.dram_tensor("wdt", [FH, D], I32, kind="ExternalInput")
    coef_d = nc.dram_tensor("coef", [128, 1], F32, kind="ExternalInput")
    out_d = nc.dram_tensor("out", [D, T], F32, kind="ExternalOutput")
    cc_in = nc.dram_tensor("cc_in", [128, 520], F32)
    h_dram = nc.dram_tensor("h_dram", [FH, T], BF16)
    cc_out = nc.dram_tensor("cc_out", [256, 520], F32)

    with ExitStack() as ctx:
        sb = lambda n, s, d: ctx.enter_context(nc.sbuf_tensor(n, s, d))
        fw = FW(nc, ctx)

        ident = sb("ident", [128, 128], BF16)
        ident32 = sb("ident32", [128, 128], F32)
        ones_c = sb("ones_c", [128, 1], BF16)
        onesr = sb("onesr", [1, 128], BF16)
        c1k = sb("c1k", [1, 128], BF16)
        ones1f = sb("ones1f", [1, 128], F32)
        bias_sb = sb("bias_sb", [128, 32], F32)
        bvb = sb("bvb", [128, D], BF16)
        coef_sb = sb("coef_sb", [128, 1], F32)
        xn = sb("xn", [128, NF * T], BF16)
        wgsb = [sb(f"wgsb{b}", [128, D], BF16) for b in range(2)]
        wusb = [sb(f"wusb{b}", [128, D], BF16) for b in range(2)]
        S_run = sb("S_run", [128, 520], F32)
        epsc = sb("epsc", [1, 1], F32)
        tblw = sb("tblw", [1, 1], F32)
        S_runb = sb("S_runb", [128, 520], BF16)

        block = ctx.enter_context(nc.Block())

        cBVR = ExitStack()
        bvr = cBVR.enter_context(nc.sbuf_tensor("bvr", [1, D], F32))
        cPSBV = ExitStack()
        ps_bv = cPSBV.enter_context(nc.psum_tensor("ps_bv", [128, 512], F32))

        e_bias = fw.dma("sp", lambda s: s.dma_start(out=bias_sb[:, :], in_=bias_d[:, :]), sem="ld_bias")
        e_coef = fw.dma("sp", lambda s: s.dma_start(out=coef_sb[:, :], in_=coef_d[:, :]), sem="ld_coef")
        e_ic0 = fw.emit("pool", lambda g: g.memset(ident[:, :], 1.0), inc="pool")
        e_id = fw.emit("pool", lambda g: g.affine_select(
            ident[:, :], ident[:, :], [[1, 128]], ALU.is_equal, 0.0,
            base=0, channel_multiplier=-1), wait=e_ic0, inc="pool")
        e_i320 = fw.emit("pool", lambda g: g.memset(ident32[:, :], 1.0), inc="pool")
        e_id32 = fw.emit("pool", lambda g: g.affine_select(
            ident32[:, :], ident32[:, :], [[1, 128]], ALU.is_equal, 0.0,
            base=0, channel_multiplier=-1), wait=e_i320, inc="pool")
        e_oc = fw.emit("dve", lambda v: v.memset(ones_c[:, :], 1.0), inc="dve")
        e_eps = fw.emit("dve", lambda v: v.memset(epsc[:, :], EPS), inc="dve")
        e_or = fw.emit("dve", lambda v: v.memset(onesr[:, :], 1.0), inc="dve")
        e_c1k = fw.emit("dve", lambda v: v.memset(c1k[:, :], 1.0 / D), inc="dve")
        e_o1f = fw.emit("dve", lambda v: v.memset(ones1f[:, :], 1.0), inc="dve")
        fw.emit("act", lambda sc: sc.activation(
            out=tblw[:, :], in_=epsc[0:1, 0:1], func=AF.Ln, scale=1.0),
            wait=[e_eps], inc="act")

        e_bvr = fw.dma("sp", lambda s: s.dma_start(out=bvr[:, :], in_=bvrow_d[:, :]), sem="ld_bvr")
        e_bvbc = []
        for half in range(2):
            e_mm = fw.emit("pe", lambda t, half=half: t.matmul(
                ps_bv[:, 0:512], ones1f[:, :], bvr[:, 512 * half:512 * half + 512],
                start=True, stop=True),
                wait=[e_o1f, e_bvr] + e_bvbc[-1:], inc="pe")
            e_ev = fw.emit("dve", lambda v, half=half: v.tensor_copy(
                bvb[:, 512 * half:512 * half + 512], ps_bv[:, 0:512]),
                wait=e_mm, inc="dve")
            e_bvbc.append(e_ev)
        cPSBV.close()
        cBVR.close()


        cS2 = ExitStack()
        sb2 = lambda n, s, d: cS2.enter_context(nc.sbuf_tensor(n, s, d))
        kfm = sb2("kfm", [128, NF * T], BF16)
        vtm = sb2("vtm", [128, nCH * 1040], BF16)
        qfm = sb2("qfm", [128, NF * T], BF16)
        wsb = [sb2(f"wsb{b}", [128, NF * D], BF16) for b in range(2)]
        ktm = [sb2(f"ktm{b}", [128, D], BF16) for b in range(2)]

        cS1 = ExitStack()
        sb1 = lambda n, s, d: cS1.enter_context(nc.sbuf_tensor(n, s, d))
        bfx = [sb1(f"bfx{b}", [128, NF * QW], BF16) for b in range(2)]
        sqr = [sb1(f"sqr{b}", [128, QW], BF16) for b in range(2)]
        rows = [sb1(f"rows{b}", [1, 4 * QW], F32) for b in range(2)]
        rowb = [sb1(f"rowb{b}", [1, 2 * QW], BF16) for b in range(2)]
        tmp = [sb1(f"tmp{b}", [128, QW], F32) for b in range(2)]
        phie = [sb1(f"phie{b}", [128, 512], BF16) for b in range(2)]
        phir = [sb1(f"phir{b}", [128, 512], BF16) for b in range(2)]
        S_in = sb1("S_in", [128, 520], F32)

        e_vone = fw.emit("dve", lambda v: v.memset(
            bass.AP(vtm, 64, [[nCH * 1040, 128], [1040, nCH], [65, H], [1, 1]]), 1.0),
            inc="dve")

        cPS2 = ExitStack()
        ps2f = lambda n, s, d: cPS2.enter_context(nc.psum_tensor(n, s, d))
        ps_st = ps2f("ps_st", [1, 512], F32)
        ps_mr = ps2f("ps_mr", [128, 512], F32)
        ps_p = [ps2f(f"ps_p{i}", [128, 512], F32) for i in range(2)]
        ps_S = ps2f("ps_S", [128, 512], F32)
        ps_S2 = ps2f("ps_S2", [128, 512], F32)
        ps_ktm = ps2f("ps_ktm", [128, 1024], BF16)

        def wload(dram, slot, semname, war=None):
            return fw.dma("pool", lambda g, slot=slot, dram=dram: g.dma_start(
                out=wsb[slot][:, :],
                in_=bass.AP(dram, 0, [[D, 128], [128 * D, NF], [1, D]])),
                wait=war, sem=semname)

        ld_ev = [None] * nQ
        ap_ev = [None] * nQ
        row_ev = [None] * nQ
        rowwar_ev = [None] * nQ
        bc_ev = [None] * nQ
        s2mm_ev = {}

        def emit_p1_quarter(qi):
            b = qi % 2
            cons = (ap_ev[qi - 2] + rowwar_ev[qi - 2]) if qi >= 2 else None
            ld_ev[qi] = fw.dma("pool", lambda g, b=b, qi=qi: g.dma_start(
                out=bfx[b][:, :],
                in_=bass.AP(x_d, QW * qi, [[T, 128], [128 * T, NF], [1, QW]])),
                wait=cons, sem=f"bfx{b}")
            e_s1 = None
            for j in range(NF):
                ws = [ld_ev[qi], e_oc] + e_bvbc[-1:]
                if qi >= 1:
                    ws += rowwar_ev[qi - 1]
                e_s1 = fw.emit("pe", lambda t, j=j, b=b: t.matmul(
                    ps_st[:, 0:QW], ones_c[:, :], bfx[b][:, QW * j:QW * (j + 1)],
                    start=(j == 0), stop=(j == NF - 1), skip_group_check=True),
                    wait=ws, inc="pe" if j == NF - 1 else None)
                sq_war = ([s2mm_ev[(qi, j - 2)]] if j >= 2 else
                          ([s2mm_ev[(qi - 1, NF - 2 + j)]] if qi >= 1 else []))
                e_sq = fw.emit("act", lambda sc, j=j, b=b: sc.activation(
                    out=sqr[j % 2][:, :], in_=bfx[b][:, QW * j:QW * (j + 1)],
                    func=AF.Square),
                    wait=[ld_ev[qi]] + sq_war, inc="act")
                s2mm_ev[(qi, j)] = fw.emit("pe", lambda t, j=j: t.matmul(
                    ps_st[:, QW:2 * QW], ones_c[:, :], sqr[j % 2][:, :],
                    start=False, stop=(j == NF - 1), skip_group_check=True),
                    wait=e_sq, inc="pe")
            e_s2 = s2mm_ev[(qi, NF - 1)]
            R = rows[b]
            e_r1 = fw.emit("dve", lambda v, R=R: v.tensor_copy(R[:, 0:QW], ps_st[:, 0:QW]),
                           wait=[e_s1, e_s2] + (rowwar_ev[qi - 2] if qi >= 2 else []), inc="dve")
            e_r2 = fw.emit("dve", lambda v, R=R: v.tensor_copy(R[:, QW:2 * QW], ps_st[:, QW:2 * QW]),
                           wait=[e_s2], inc="dve")
            e_a = fw.emit("dve", lambda v, R=R: v.tensor_mul(
                R[:, 2 * QW:3 * QW], R[:, 0:QW], R[:, 0:QW]), wait=[e_r1], inc="dve")
            e_b = fw.emit("dve", lambda v, R=R: v.scalar_tensor_tensor(
                R[:, 3 * QW:4 * QW], R[:, 2 * QW:3 * QW], -1.0 / D, R[:, QW:2 * QW],
                ALU.mult, ALU.add), wait=[e_a, e_r2], inc="dve")
            e_l = fw.emit("act", lambda sc, R=R: sc.activation(
                out=R[:, 2 * QW:3 * QW], in_=R[:, 3 * QW:4 * QW], func=AF.Ln,
                scale=1.0 / D, bias=epsc[0:1, 0:1]), wait=[e_b, e_eps], inc="act")
            e_rs = fw.emit("act", lambda sc, R=R: sc.activation(
                out=R[:, 3 * QW:4 * QW], in_=R[:, 2 * QW:3 * QW], func=AF.Exp,
                scale=-0.5), wait=e_l, inc="act")
            RB = rowb[b]
            e_m16 = fw.emit("dve", lambda v, R=R, RB=RB: v.tensor_copy(RB[:, 0:QW], R[:, 0:QW]),
                            wait=[e_r1] + ([bc_ev[qi - 2]] if qi >= 2 else []), inc="dve")
            e_rb16 = fw.emit("dve", lambda v, R=R, RB=RB: v.tensor_copy(
                RB[:, QW:2 * QW], R[:, 3 * QW:4 * QW]), wait=[e_rs], inc="dve")
            row_ev[qi] = e_rs
            rowwar_ev[qi] = [e_rs, e_m16, e_rb16]
            war = ap_ev[qi - 1] if qi >= 1 else []
            e_bm = fw.emit("pe", lambda t, RB=RB: t.matmul(
                ps_mr[:, 0:QW], c1k[:, :], RB[:, 0:QW],
                start=True, stop=True, skip_group_check=True),
                wait=[e_m16, e_c1k] + war, inc="pe")
            e_br = fw.emit("pe", lambda t, RB=RB: t.matmul(
                ps_mr[:, QW:2 * QW], onesr[:, :], RB[:, QW:2 * QW],
                start=False, stop=True, skip_group_check=True),
                wait=[e_rb16, e_or], inc="pe")
            evs = []
            for j in range(NF):
                e_sub = fw.emit("dve", lambda v, j=j, b=b: v.tensor_sub(
                    tmp[j % 2][:, :], bfx[b][:, QW * j:QW * (j + 1)], ps_mr[:, 0:QW]),
                    wait=[e_bm] + ([evs[j - 2]] if j >= 2 else []), inc="dve")
                e_mu = fw.emit("dve", lambda v, j=j, qi=qi: v.tensor_mul(
                    xn[:, j * T + QW * qi: j * T + QW * (qi + 1)],
                    tmp[j % 2][:, :], ps_mr[:, QW:2 * QW]),
                    wait=[e_sub, e_br], inc="dve")
                evs.append(e_mu)
            ap_ev[qi] = evs
            bc_ev[qi] = evs[-1]

        p_ev = [None, None]
        vch_ev = [None] * nCH
        vstate = {"last_vmm": None}
        kph_ev = {}
        trstate = {"ktm_war": [None, None], "ktm_bank": None, "last_Smm": None}

        def emit_v(i):
            ve = []
            for oh in range(2):
                sl = oh
                qi0 = (i * CK) // QW
                need = [ev for q in range(qi0, qi0 + max(CK // QW, 1)) for ev in ap_ev[q]]
                def vmm(t, i=i, oh=oh, sl=sl):
                    r = None
                    for kk in range(NF):
                        r = t.matmul(ps_p[sl][:, 0:512],
                                     xn[:, kk * T + CK * i: kk * T + CK * (i + 1)],
                                     wsb[0][:, D * kk + 512 * oh: D * kk + 512 * oh + 512],
                                     start=(kk == 0), stop=(kk == NF - 1))
                    return r
                vstate["last_vmm"] = fw.emit("pe", vmm,
                                             wait=[e_wv, p_ev[sl]] + need, inc="pe")
                e_evac = fw.emit("dve", lambda v, i=i, oh=oh, sl=sl: v.tensor_add(
                    bass.AP(vtm, i * 1040 + 65 * 8 * oh, [[nCH * 1040, 128], [65, 8], [1, 64]]),
                    ps_p[sl][:, 0:512], bvb[:, 512 * oh:512 * oh + 512]),
                    wait=[vstate["last_vmm"]] + e_bvbc[-1:], inc="dve")
                p_ev[sl] = e_evac
                ve.append(e_evac)
            vch_ev[i] = ve

        def emit_k(t4):
            for och in range(NF):
                sl = och % 2
                qi0 = (t4 * 512) // QW
                need = [ev for q in range(qi0, qi0 + 512 // QW) for ev in ap_ev[q]]
                def kmm(t, och=och, t4=t4, sl=sl):
                    r = None
                    for kk in range(NF):
                        r = t.matmul(ps_p[sl][:, 0:512],
                                     wsb[1][:, D * kk + 128 * och: D * kk + 128 * och + 128],
                                     xn[:, kk * T + 512 * t4: kk * T + 512 * (t4 + 1)],
                                     start=(kk == 0), stop=(kk == NF - 1))
                    return r
                e_mm = fw.emit("pe", kmm, wait=[e_wk, p_ev[sl]] + need, inc="pe")
                e_e = fw.emit("act", lambda sc, och=och, sl=sl: sc.activation(
                    out=phie[sl][:, :], in_=ps_p[sl][:, 0:512], func=AF.Exp,
                    bias=bias_sb[:, 8 + och:9 + och]),
                    wait=[e_mm, e_bias], inc="act")
                e_r = fw.emit("act", lambda sc, och=och, sl=sl: sc.activation(
                    out=phir[sl][:, :], in_=ps_p[sl][:, 0:512], func=AF.Relu,
                    bias=bias_sb[:, 8 + och:9 + och]),
                    wait=[e_mm], inc="act")
                e_phi = fw.emit("dve", lambda v, och=och, t4=t4, sl=sl: v.scalar_tensor_tensor(
                    kfm[:, och * T + 512 * t4: och * T + 512 * (t4 + 1)],
                    phie[sl][:, :], 1.0, phir[sl][:, :], ALU.min, ALU.add),
                    wait=[e_e, e_r], inc="dve")
                p_ev[sl] = e_phi
                kph_ev[(och, t4)] = e_phi

        def emit_trS(i):
            t4 = (i * CK) // 512
            kneed = [kph_ev[(och, t4)] for och in range(NF)]
            def ktr(t, i=i):
                r = None
                for och in range(NF):
                    r = t.transpose(ps_ktm[:, 128 * och:128 * (och + 1)],
                                    kfm[:, och * T + CK * i: och * T + CK * (i + 1)],
                                    ident[:, :])
                return r
            e_tr = fw.emit("pe", ktr,
                           wait=kneed + [e_id, trstate["ktm_war"][i % 2], trstate["ktm_bank"]],
                           inc="pe")
            e_kev = fw.emit("dve", lambda v, i=i: v.tensor_copy(ktm[i % 2][:, :], ps_ktm[:, :]),
                            wait=e_tr, inc="dve")
            trstate["ktm_war"][i % 2] = e_kev
            trstate["ktm_bank"] = e_kev
            for h in range(H):
                blk, off = sblock(h)
                pst = ps_S2 if blk else ps_S
                pb = (h % 2) * 64
                first = (i == 0) and (h in (0, 1, 14, 15))
                trstate["last_Smm"] = fw.emit("pe", lambda t, i=i, h=h, pst=pst, off=off, pb=pb, first=first: t.matmul(
                    pst[pb:pb + 64, off:off + 65],
                    ktm[i % 2][:, DH * h:DH * (h + 1)],
                    bass.AP(vtm, i * 1040 + 65 * h, [[nCH * 1040, 128], [1, 65]]),
                    start=first, stop=True, skip_group_check=True,
                    tile_position=(0, pb)),
                    wait=[e_kev, vch_ev[i][h // 8], e_vone],
                    inc="pe" if (i == nCH - 1 and h == H - 1) else None)

        emit_p1_quarter(0)
        emit_p1_quarter(1)
        e_wv = wload(wvt_d, 0, "ld_wv")
        e_wk = wload(wkt_d, 1, "ld_wk")
        for qp in range(nQ // 2):
            if qp + 1 < nQ // 2:
                emit_p1_quarter(2 * qp + 2)
                emit_p1_quarter(2 * qp + 3)
            for i in range(4 * qp, min(4 * qp + 4, nCH)):
                emit_v(i)
            emit_k(qp)
            for i in range(4 * qp, min(4 * qp + 4, nCH)):
                emit_trS(i)
        p1_psum_done = ap_ev[nQ - 1] + rowwar_ev[nQ - 1] + \
            (rowwar_ev[nQ - 2] if nQ >= 2 else []) + e_bvbc[-1:]
        last_vmm = vstate["last_vmm"]
        ktm_war = trstate["ktm_war"]
        last_Smm = trstate["last_Smm"]

        e_wq = wload(wqt_d, 0, "ld_wq", war=[last_vmm])
        e_sl1 = fw.emit("dve", lambda v: v.tensor_copy(S_run[:, 0:455], ps_S[:, 0:455]),
                        wait=last_Smm, inc="dve")
        e_sl2 = fw.emit("dve", lambda v: v.tensor_copy(S_run[:, 455:520], ps_S2[:, 0:65]),
                        wait=last_Smm, inc="dve")
        e_ccs = fw.dma("sp", lambda s: s.dma_start(out=cc_in[:, :], in_=S_run[:, :]),
                       wait=[e_sl1, e_sl2], sem="st_cc")
        e_cc = fw.emit("pool", lambda g: g.collective_compute(
            "AllGather", ALU.bypass,
            replica_groups=[[0, 1], [2, 3], [4, 5], [6, 7]],
            ins=[cc_in.ap().opt()], outs=[cc_out.ap().opt()]),
            wait=e_ccs, inc="cc")

        for och in range(NF):
            for t4 in range(n512):
                sl = (och * n512 + t4) % 2
                def qmm(t, och=och, t4=t4, sl=sl):
                    r = None
                    for kk in range(NF):
                        r = t.matmul(ps_p[sl][:, 0:512],
                                     wsb[0][:, D * kk + 128 * och: D * kk + 128 * och + 128],
                                     xn[:, kk * T + 512 * t4: kk * T + 512 * (t4 + 1)],
                                     start=(kk == 0), stop=(kk == NF - 1))
                    return r
                e_mm = fw.emit("pe", qmm, wait=[e_wq, p_ev[sl]], inc="pe")
                e_e = fw.emit("act", lambda sc, och=och, sl=sl: sc.activation(
                    out=phie[sl][:, :], in_=ps_p[sl][:, 0:512], func=AF.Exp,
                    bias=bias_sb[:, och:och + 1]),
                    wait=[e_mm], inc="act")
                e_r = fw.emit("act", lambda sc, och=och, sl=sl: sc.activation(
                    out=phir[sl][:, :], in_=ps_p[sl][:, 0:512], func=AF.Relu,
                    bias=bias_sb[:, och:och + 1]),
                    wait=[e_mm], inc="act")
                e_phi = fw.emit("dve", lambda v, och=och, t4=t4, sl=sl: v.scalar_tensor_tensor(
                    qfm[:, och * T + 512 * t4: och * T + 512 * (t4 + 1)],
                    phie[sl][:, :], 1.0, phir[sl][:, :], ALU.min, ALU.add),
                    wait=[e_e, e_r], inc="dve")
                p_ev[sl] = e_phi
                kph_ev[("q", och, t4)] = e_phi
        e_wo = wload(wot_d, 1, "ld_wo", war=[kph_ev[(NF - 1, n512 - 1)]])
        q_done = [kph_ev[("q", och, t4)] for och in range(NF) for t4 in range(n512)]
        e_sin = fw.dma("sp", lambda s: s.dma_start(out=S_in[:, :], in_=cc_out[0:128, :]),
                       wait=e_cc, sem="ld_sin")
        e_sr = fw.emit("dve", lambda v: v.tensor_single_scalar(
            S_run[:, :], S_in[:, :], coef_sb[:, 0:1], ALU.mult),
            wait=[e_sin, e_coef, e_ccs], inc="dve")
        e_srb = fw.emit("dve", lambda v: v.tensor_copy(S_runb[:, :], S_run[:, :]),
                        wait=e_sr, inc="dve")
        wg_ev = [None, None]
        wu_ev = [None, None]
        for j in range(2):
            wg_ev[j] = fw.dma("pool", lambda g, j=j: g.dma_start(
                out=wgsb[j][:, :],
                in_=bass.AP(wgt_d, 128 * j, [[FH, 128], [128 * FH, NF], [1, 128]])),
                sem=f"ld_wg{j}")
            wu_ev[j] = fw.dma("pool", lambda g, j=j: g.dma_start(
                out=wusb[j][:, :],
                in_=bass.AP(wut_d, 128 * j, [[FH, 128], [128 * FH, NF], [1, 128]])),
                sem=f"ld_wu{j}")
        cPS2.close()
        cS1.close()

        cS4 = ExitStack()
        sb4 = lambda n, s, d: cS4.enter_context(nc.sbuf_tensor(n, s, d))
        cPS4 = ExitStack()
        ps4f = lambda n, s, d: cPS4.enter_context(nc.psum_tensor(n, s, d))
        sTs = sb4("sTs", [128, 512], BF16)
        mask2 = sb4("mask2", [128, 256], BF16)
        A_tm = [sb4(f"A_tm{b}", [128, D], F32) for b in range(2)]
        A_fm = sb4("A_fm", [128, NF * 512], BF16)
        rcp = sb4("rcp", [128, 4], F32)
        ost = [sb4(f"ost{b}", [128, 512], F32) for b in range(2)]
        ps_sT = ps4f("ps_sT", [128, 1024], F32)
        ps_nm = ps4f("ps_nm", [128, 1024], F32)
        ps_fg = ps4f("ps_fg", [128, 512], F32)
        ps_fu = ps4f("ps_fu", [128, 512], F32)
        ps_kt4 = ps4f("ps_kt4", [128, 1024], BF16)
        ps_w = ps4f("ps_w", [128, 512], F32)
        WH = min(512, T)
        hbuf = [sb4(f"hbuf{b}", [128, WH], BF16) for b in range(2)]
        sgf = sb4("sgf", [128, WH], BF16)

        e_mk0 = fw.emit("pool", lambda g: g.memset(mask2[:, :], 1.0), inc="pool")
        e_mk = fw.emit("pool", lambda g: g.affine_select(
            mask2[:, :], mask2[:, :], [[0, 2], [1, 128]], ALU.is_ge, 0.0,
            base=0, channel_multiplier=-1), wait=e_mk0, inc="pool")

        scope_bar = [e_sl1, e_sl2, e_srb] + p1_psum_done + q_done + [p_ev[0], p_ev[1]]

        THF = T
        n4h = max(THF // 512, 1)

        class GUFiller:
            def __init__(self):
                self.seq = []
                for j in range(NJ):
                    for t4 in range(n4h):
                        self.seq.append((j, "g", t4))
                        self.seq.append((j, "u", t4))
                self.gi = 0
                self.kk = 0
                self.g_war = e_sl1
                self.u_war = e_sl2
                self.sg_war = None
                self.h_war = [None, None]
                self.gu_done = {}
                self.wev = {0: wg_ev[0], 1: wg_ev[1]}
                self.wuev = {0: wu_ev[0], 1: wu_ev[1]}
                self.hst = []
                self.last_mm = None
                self.first_bar = list(q_done) + p1_psum_done + [e_srb]
                self.nsg = 0
                self.nh = 0

            def done(self):
                return self.gi >= len(self.seq)

            def step(self, n):
                for _ in range(n):
                    if self.done():
                        return
                    j, kind, t4 = self.seq[self.gi]
                    b = j % 2
                    kk = self.kk
                    ws = None
                    if kk == 0:
                        ws = [self.wev[b] if kind == "g" else self.wuev[b],
                              self.g_war if kind == "g" else self.u_war]
                    wv = wgsb[b] if kind == "g" else wusb[b]
                    ps = ps_fg if kind == "g" else ps_fu
                    ev = fw.emit("pe", lambda t, kk=kk, t4=t4, wv=wv, ps=ps: t.matmul(
                        ps[:, 0:WH], wv[:, 128 * kk:128 * (kk + 1)],
                        xn[:, kk * T + WH * t4: kk * T + WH * (t4 + 1)],
                        start=(kk == 0), stop=(kk == NF - 1)),
                        wait=ws, inc="pe" if kk == NF - 1 else None)
                    self.kk += 1
                    if self.kk == NF:
                        self.kk = 0
                        self.gi += 1
                        if kind == "g":
                            e_sg = fw.emit("act", lambda sc: sc.activation(
                                out=sgf[:, :], in_=ps_fg[:, 0:WH], func=AF.Silu,
                                scale=QSC),
                                wait=[ev, self.sg_war] +
                                     (self.first_bar if self.nsg == 0 else []),
                                inc="act")
                            self.nsg += 1
                            self.g_war = e_sg
                            self.pend_sg = e_sg
                        else:
                            hsl = (j * n4h + t4) % 2
                            e_h = fw.emit("dve", lambda v, hsl=hsl: v.tensor_mul(
                                hbuf[hsl][:, :], sgf[:, :], ps_fu[:, 0:WH]),
                                wait=[ev, self.pend_sg, self.h_war[hsl]] +
                                     (self.first_bar if self.nh < 2 else []), inc="dve")
                            self.nh += 1
                            self.u_war = e_h
                            self.sg_war = e_h
                            self.gu_done[j] = e_h
                            e_st = fw.dma("sp", lambda s, j=j, t4=t4, hsl=hsl: s.dma_start(
                                out=h_dram[128 * j:128 * (j + 1), WH * t4:WH * (t4 + 1)],
                                in_=hbuf[hsl][:, :]),
                                wait=e_h, sem=f"hst{hsl}")
                            self.h_war[hsl] = e_st
                            self.hst.append(e_st)
                            if t4 == n4h - 1 and j + 2 < NJ:
                                jn = j + 2
                                self.wev[jn % 2] = fw.dma("pool", lambda g, jn=jn: g.dma_start(
                                    out=wgsb[jn % 2][:, :],
                                    in_=bass.AP(wgt_d, 128 * jn, [[FH, 128], [128 * FH, NF], [1, 128]])),
                                    wait=[self.gu_done[jn - 2]], sem=f"ld_wg{jn % 2}")
                                self.wuev[jn % 2] = fw.dma("pool", lambda g, jn=jn: g.dma_start(
                                    out=wusb[jn % 2][:, :],
                                    in_=bass.AP(wut_d, 128 * jn, [[FH, 128], [128 * FH, NF], [1, 128]])),
                                    sem=f"ld_wu{jn % 2}")

        guf = GUFiller()
        guf.step(96)
        sd_red = None
        sd_wr = None
        kt4_ev = None
        psw_ev = None
        wo_nm_ev = None
        afm_ev = {}
        msk_ev = {}
        rc_ev = {}
        dv_ev = {}
        m3e = {}
        ost_war = [None, None]
        upd_evs = [e_srb]
        for i in range(nCH):
            gslot = i % 4
            if i > 0:
                e_u1 = fw.emit("dve", lambda v: v.tensor_add(
                    S_run[:, 0:455], S_run[:, 0:455], ps_sT[:, 0:455]),
                    wait=[sd_wr], inc="dve")
                e_u2 = fw.emit("dve", lambda v: v.tensor_add(
                    S_run[:, 455:520], S_run[:, 455:520], ps_sT[:, 512:577]),
                    wait=[sd_wr], inc="dve")
                e_ub = fw.emit("dve", lambda v: v.tensor_copy(S_runb[:, :], S_run[:, :]),
                               wait=[e_u1, e_u2], inc="dve")
                sd_red = e_ub
                upd_evs = [e_ub]
            def ktr4(t, i=i):
                r = None
                for och in range(NF):
                    r = t.transpose(ps_kt4[:, 128 * och:128 * (och + 1)],
                                    kfm[:, och * T + CK * i: och * T + CK * (i + 1)],
                                    ident[:, :])
                return r
            e_tr4 = fw.emit("pe", ktr4,
                            wait=scope_bar + [ktm_war[i % 2], kt4_ev], inc="pe")
            e_k4 = fw.emit("dve", lambda v, i=i: v.tensor_copy(ktm[i % 2][:, :], ps_kt4[:, :]),
                           wait=e_tr4, inc="dve")
            kt4_ev = e_k4
            ktm_war[i % 2] = e_k4
            for p in range(H // 2):
                gp = i * (H // 2) + p
                m1e = None
                for hh in range(2):
                    h = 2 * p + hh
                    pb = (h % 2) * 64
                    m1e = fw.emit("pe", lambda t, i=i, h=h, pb=pb, hh=hh: t.matmul(
                        ps_sT[:, 512 * hh:512 * hh + 128],
                        kfm[pb:pb + 64, (h // 2) * T + CK * i: (h // 2) * T + CK * (i + 1)],
                        qfm[pb:pb + 64, (h // 2) * T + CK * i: (h // 2) * T + CK * (i + 1)],
                        start=True, stop=True, skip_group_check=True),
                        wait=scope_bar + [msk_ev.get(gp - 1)] + (upd_evs if p == 0 else []),
                        inc="pe" if hh == 1 else None)
                guf.step(9)
                e_m = fw.emit("dve", lambda v, p=p, gp=gp: v.tensor_mul(
                    sTs[:, 256 * (gp % 2):256 * (gp % 2) + 256],
                    bass.AP(ps_sT, 0, [[1024, 128], [512, 2], [1, 128]]),
                    mask2[:, :]),
                    wait=[m1e, e_mk, m3e.get(gp - 2)], inc="dve")
                msk_ev[gp] = e_m
                for hh in range(2):
                    h = 2 * p + hh
                    pb = (h % 2) * 64
                    fw.emit("pe", lambda t, i=i, h=h, hh=hh, gp=gp: t.matmul(
                        ps_nm[:, 512 * hh:512 * hh + 65],
                        sTs[:, 256 * (gp % 2) + 128 * hh:256 * (gp % 2) + 128 * hh + 128],
                        bass.AP(vtm, i * 1040 + 65 * h, [[nCH * 1040, 128], [1, 65]]),
                        start=True, stop=False, skip_group_check=True),
                        wait=[e_m, rc_ev.get(gp - 1), dv_ev.get(2 * (gp - 1) + hh),
                              wo_nm_ev if hh == 0 else None])
                    ev3 = fw.emit("pe", lambda t, h=h, i=i, pb=pb, hh=hh: t.matmul(
                        ps_nm[:, 512 * hh:512 * hh + 65],
                        qfm[pb:pb + 64, (h // 2) * T + CK * i: (h // 2) * T + CK * (i + 1)],
                        S_runb[pb:pb + 64, 65 * (h // 2):65 * (h // 2) + 65],
                        start=False, stop=True, skip_group_check=True),
                        wait=upd_evs, inc="pe" if hh == 1 else None)
                m3e[gp] = ev3
                e_rc = fw.emit("dve", lambda v, gp=gp: v.reciprocal(
                    rcp[:, 2 * (gp % 2):2 * (gp % 2) + 2],
                    bass.AP(ps_nm, 64, [[1024, 128], [512, 2], [1, 1]])),
                    wait=[ev3, dv_ev.get(2 * (gp - 2) + 1)], inc="dve")
                rc_ev[gp] = e_rc
                for hh in range(2):
                    h = 2 * p + hh
                    dv_ev[2 * gp + hh] = fw.emit("act", lambda sc, i=i, h=h, hh=hh, gp=gp: sc.activation(
                        out=A_tm[i % 2][:, DH * h:DH * (h + 1)],
                        in_=ps_nm[:, 512 * hh:512 * hh + 64],
                        func=AF.Identity, scale=rcp[:, 2 * (gp % 2) + hh:2 * (gp % 2) + hh + 1]),
                        wait=[e_rc, afm_ev.get((i - 2, 1))], inc="act")
            last_msk = msk_ev[i * (H // 2) + (H // 2) - 1]
            for h in range(H):
                blk, off = sblock(h)
                off = 512 if blk else off
                pb = (h % 2) * 64
                e_sd = fw.emit("pe", lambda t, i=i, h=h, off=off, pb=pb: t.matmul(
                    ps_sT[pb:pb + 64, off:off + 65],
                    ktm[i % 2][:, DH * h:DH * (h + 1)],
                    bass.AP(vtm, i * 1040 + 65 * h, [[nCH * 1040, 128], [1, 65]]),
                    start=True, stop=True, skip_group_check=True,
                    tile_position=(0, pb)),
                    wait=[kt4_ev, sd_red, last_msk], inc="pe" if h == H - 1 else None)
                if h == H - 1:
                    sd_wr = e_sd
            guf.step(2)
            chunk_divs = [dv_ev[2 * (i * (H // 2) + (H // 2) - 1) + 1]]
            for w in range(2):
                def atr(t, i=i, w=w):
                    r = None
                    for o in range(4):
                        och = 4 * w + o
                        r = t.transpose(ps_w[:, 128 * o:128 * (o + 1)],
                                        A_tm[i % 2][:, 128 * och:128 * (och + 1)],
                                        ident32[:, :])
                    return r
                e_at = fw.emit("pe", atr,
                               wait=chunk_divs + [psw_ev, wo_nm_ev, e_id32], inc="pe")
                e_ae = fw.emit("dve", lambda v, i=i, w=w, gslot=gslot: v.tensor_copy(
                    bass.AP(A_fm, 512 * 4 * w + 128 * gslot,
                            [[NF * 512, 128], [512, 4], [1, 128]]),
                    ps_w[:, 0:512]),
                    wait=e_at, inc="dve")
                psw_ev = e_ae
                afm_ev[(i, w)] = e_ae
            if gslot == 3:
                g4 = i // 4
                aneed = [afm_ev[(4 * g4 + c, w)] for c in range(4) for w in range(2)]
                last_gp = i * (H // 2) + (H // 2) - 1
                for och in range(NF):
                    if och % 2 == 0:
                        wob = ps_w[:, 0:512]
                        war = [psw_ev]
                    else:
                        wob = ps_nm[:, 0:512]
                        war = [rc_ev[last_gp], dv_ev[2 * last_gp], wo_nm_ev]
                    def womm(t, och=och, wob=wob):
                        r = None
                        for kk in range(NF):
                            r = t.matmul(wob,
                                         wsb[1][:, D * kk + 128 * och: D * kk + 128 * och + 128],
                                         A_fm[:, kk * 512:(kk + 1) * 512],
                                         start=(kk == 0), stop=(kk == NF - 1),
                                         skip_group_check=True)
                        return r
                    e_wm = fw.emit("pe", womm, wait=aneed + [e_wo] + war, inc="pe")
                    e_oev = fw.emit("dve", lambda v, och=och, wob=wob: v.tensor_scalar(
                        ost[och % 2][:, :], wob, bias_sb[:, 24 + och:25 + och],
                        None, ALU.add),
                        wait=[e_wm, e_bias, ost_war[och % 2]], inc="dve")
                    if och % 2 == 0:
                        psw_ev = e_oev
                    else:
                        wo_nm_ev = e_oev
                    ost_war[och % 2] = fw.dma("sp", lambda s, och=och, g4=g4: s.dma_start(
                        out=out_d[128 * och:128 * (och + 1), 512 * g4:512 * (g4 + 1)],
                        in_=ost[och % 2][:, :]),
                        wait=e_oev, sem=f"st_attn{och % 2}")
        n_attn_st = NF * nG
        last_gp_all = nCH * (H // 2) - 1
        p4_scan = [wo_nm_ev, psw_ev, kt4_ev, upd_evs[-1], sd_wr,
                   msk_ev[last_gp_all], rc_ev[last_gp_all],
                   dv_ev[2 * last_gp_all], dv_ev[2 * last_gp_all + 1]]
        cPS4.close()
        cS4.close()
        cS2.close()

        cS3 = ExitStack()
        sb3 = lambda n, s, d: cS3.enter_context(nc.sbuf_tensor(n, s, d))
        cPS3 = ExitStack()
        ps3f = lambda n, s, d: cPS3.enter_context(nc.psum_tensor(n, s, d))
        hfull = sb3("hfull", [128, NJ * T], BF16)
        wdsb = sb3("wdsb", [128, NJ * D], BF16)
        x3 = [sb3(f"x3{b}", [128, W4], F32) for b in range(2)]
        ost3 = [sb3(f"ost3{b}", [128, W4], F32) for b in range(2)]
        ps_d = [ps3f(f"ps_d{b}", [128, 512], F32) for b in range(2)]

        wd_parts = []
        h_parts = []

        def emit_wd():
            p = len(wd_parts)
            pw = [] if p == 0 else [guf.hst[-1]]
            wd_parts.append(fw.dma("pool", lambda g, p=p: g.dma_start(
                out=wdsb[:, D * 3 * p:D * 3 * (p + 1)],
                in_=bass.AP(wdt_d, 128 * 1024 * 3 * p,
                            [[1024, 128], [128 * 1024, 3], [1, 1024]])),
                wait=(p4_scan if p == 0 else []) + pw,
                sem=f"ld_wd{min(p // 3, 2)}"))

        def emit_h():
            p = len(h_parts)
            need = (4 * p + 4) * n4h
            hw = [guf.hst[need - 1], guf.hst[need - 2]]
            h_parts.append(fw.dma("act", lambda s, p=p: s.dma_start(
                out=hfull[:, 4 * T * p:4 * T * (p + 1)],
                in_=bass.AP(h_dram, 128 * T * 4 * p,
                            [[T, 128], [128 * T, 4], [1, T]])),
                wait=(p4_scan if p == 0 else []) + hw, sem=f"ld_h{p // 2}"))

        emit_wd()
        emit_h()
        while not guf.done():
            guf.step(16)
            if len(wd_parts) < 8:
                emit_wd()
            if (len(h_parts) < 6 and
                    (4 * len(h_parts) + 4) * n4h + 1 <= len(guf.hst)):
                emit_h()
        while len(wd_parts) < 8:
            emit_wd()
        while len(h_parts) < 6:
            emit_h()
        e_wd = wd_parts[-1]
        e_hload = h_parts[-1]

        p4_psum = p4_scan + [guf.g_war, guf.u_war]
        p4_done = list(p4_psum)
        st_attn_total = [Ev("st_attn0", fw.semcnt["st_attn0"]),
                         Ev("st_attn1", fw.semcnt["st_attn1"]),
                         Ev("hst0", fw.semcnt["hst0"]),
                         Ev("hst1", fw.semcnt["hst1"])]
        p4_done = p4_done + st_attn_total
        e_acc_last = None
        d_ev = [None, None]
        o3_ev = [None, None]

        wseg = [wd_parts[2], wd_parts[5], wd_parts[7]]
        hseg = [h_parts[1], h_parts[3], h_parts[5]]

        def emit_down(hf, hprereq, hb):
            nonlocal e_acc_last
            for t4 in range(n4):
                for kk in range(NF):
                    sl = kk % 2
                    col = hf * TH + W4 * t4
                    e_x3 = fw.dma("sp", lambda s, kk=kk, sl=sl, col=col: s.dma_start(
                        out=x3[sl][:, :],
                        in_=x_d[128 * kk:128 * (kk + 1), col:col + W4]),
                        wait=[o3_ev[sl]] if o3_ev[sl] else p4_done, sem=f"x3_{sl}")
                    e_dm = None
                    for s in range(3):
                        def dseg(t, kk=kk, sl=sl, col=col, hb=hb, s=s):
                            r = None
                            for j in range(8 * s, 8 * s + 8):
                                r = t.matmul(ps_d[sl][:, 0:W4],
                                             wdsb[:, D * j + 128 * kk: D * j + 128 * kk + 128],
                                             hb[:, j * T + col: j * T + col + W4],
                                             start=(j == 0), stop=(j == NJ - 1))
                            return r
                        e_dm = fw.emit("pe", dseg,
                                       wait=hprereq + [wseg[s], hseg[s]] +
                                            ([d_ev[sl]] if s == 0 else []),
                                       inc="pe" if s == 2 else None)
                    e_o3 = fw.emit("dve", lambda v, kk=kk, sl=sl: v.scalar_tensor_tensor(
                        ost3[sl][:, :], ps_d[sl][:, 0:W4], QSC * QSC, x3[sl][:, :],
                        ALU.mult, ALU.add),
                        wait=[e_dm, e_x3, o3_ev[sl]], inc="dve")
                    d_ev[sl] = e_o3
                    e_st = fw.dma("pool", lambda g, kk=kk, sl=sl, col=col: g.dma_start(
                        out=out_d[128 * kk:128 * (kk + 1), col:col + W4],
                        in_=ost3[sl][:, :], accum_op=ALU.add),
                        wait=[e_o3] + st_attn_total, sem=f"st_out{sl}")
                    o3_ev[sl] = e_st
                    e_acc_last = e_st

        emit_down(0, p4_done, hfull)
        emit_down(1, [], hfull)
        cPS3.close()
        cS3.close()

        out_final = [Ev("st_out0", fw.semcnt["st_out0"]), Ev("st_out1", fw.semcnt["st_out1"])]
        fw.emit("pool", lambda g: g.nop(), wait=out_final)
        fw.emit("sp", lambda s: s.nop(), wait=st_attn_total + out_final)
        fw.finish(block)
    return nc


_NC_CACHE = {}


def _get_nc(T):
    if T not in _NC_CACHE:
        _NC_CACHE[T] = build_program(T)
    return _NC_CACHE[T]


def kernel(L0_attn_x, L0_attn_Wq, L0_attn_bq, L0_attn_Wk, L0_attn_bk,
           L0_attn_Wv, L0_attn_bv, L0_attn_Wo, L0_attn_bo,
           L0_ffn_w_gate, L0_ffn_w_up, L0_ffn_w_down):
    x = np.asarray(L0_attn_x)
    B, Sq, _ = x.shape
    T = (B * Sq) // 8
    xT = np.ascontiguousarray(x.transpose(0, 2, 1))

    def wT(w):
        return np.ascontiguousarray(np.asarray(w).T)

    wqt, wkt, wvt, wot = wT(L0_attn_Wq), wT(L0_attn_Wk), wT(L0_attn_Wv), wT(L0_attn_Wo)
    wgt = np.ascontiguousarray(np.asarray(L0_ffn_w_gate).T)
    wut = np.ascontiguousarray(np.asarray(L0_ffn_w_up).T)
    wdt = np.ascontiguousarray(np.asarray(L0_ffn_w_down).T)

    def pack(bv):
        return np.ascontiguousarray(np.asarray(bv, np.float32).reshape(8, 128).T)

    bias_pack = np.ascontiguousarray(np.concatenate(
        [pack(L0_attn_bq), pack(L0_attn_bk), pack(L0_attn_bv), pack(L0_attn_bo)], axis=1))
    bv_row = np.ascontiguousarray(np.asarray(L0_attn_bv, np.float32)[None, :])

    in_maps = []
    for c in range(8):
        b, g = c // 2, c % 2
        in_maps.append({
            "x": np.ascontiguousarray(xT[b][:, g * T:(g + 1) * T]),
            "wqt": wqt, "wkt": wkt, "wvt": wvt, "wot": wot,
            "bias_pack": bias_pack, "bv_row": bv_row,
            "wgt": wgt, "wut": wut, "wdt": wdt,
            "coef": np.full((128, 1), float(g), np.float32),
        })
    nc = _get_nc(T)
    res = run_bass_kernel_spmd(nc, in_maps, core_ids=list(range(8)))
    full = np.empty((B, D, Sq), np.float32)
    for c in range(8):
        b, g = c // 2, c % 2
        full[b][:, g * T:(g + 1) * T] = res.results[c]["out"]
    return np.ascontiguousarray(full.transpose(0, 2, 1))


# version 25
# speedup vs baseline: 1.0748x; 1.0648x over previous
import numpy as np
from contextlib import ExitStack
from collections import defaultdict

import concourse.bass as bass
import concourse.mybir as mybir
from concourse.bass_utils import run_bass_kernel_spmd

F32 = mybir.dt.float32
BF16 = mybir.dt.bfloat16
I32 = mybir.dt.int32
AF = mybir.ActivationFunctionType
ALU = mybir.AluOpType

D = 1024
NF = 8
H = 16
DH = 64
CK = 128
FH = 3072
NJ = 24
QSC = 1.0 / 127.0
EPS = 1e-5
QW = 256

ENGINES = ("pe", "act", "dve", "sp", "pool")


class Ev:
    __slots__ = ("sem", "val")

    def __init__(self, sem, val):
        self.sem = sem
        self.val = val


class FW:

    def __init__(self, nc, ctx):
        self.nc = nc
        self.q = {e: [] for e in ENGINES}
        self.semcnt = defaultdict(int)
        self.hiwater = defaultdict(int)
        self.sems = {}
        self.ctx = ctx

    def sem(self, name):
        if name not in self.sems:
            self.sems[name] = self.ctx.enter_context(self.nc.semaphore("s_" + name))
        return self.sems[name]

    def wait(self, engine, evs):
        if evs is None:
            return
        if isinstance(evs, Ev):
            evs = [evs]
        best = {}
        for ev in evs:
            if ev is None:
                continue
            if ev.val > best.get(ev.sem, 0):
                best[ev.sem] = ev.val
        for semname, val in sorted(best.items()):
            if self.hiwater[(engine, semname)] >= val:
                continue
            self.hiwater[(engine, semname)] = val
            sem = self.sem(semname)
            self.q[engine].append(("wait", sem, val))

    def emit(self, engine, fn, wait=None, inc=None):
        self.wait(engine, wait)
        ev = None
        if inc is not None:
            semname, n = inc if isinstance(inc, tuple) else (inc, 1)
            self.semcnt[semname] += n
            ev = Ev(semname, self.semcnt[semname])
            sem = self.sem(semname)
            self.q[engine].append(("op", fn, sem, n))
        else:
            self.q[engine].append(("op", fn, None, 0))
        return ev

    def dma(self, engine, fn, wait=None, sem="dma"):
        return self.emit(engine, fn, wait=wait, inc=(sem, 16))

    def finish(self, block):
        def mk_body(engine):
            def body(eng):
                for item in self.q[engine]:
                    if item[0] == "wait":
                        eng.wait_ge(item[1], item[2])
                    else:
                        ins = item[1](eng)
                        if item[2] is not None:
                            ins.then_inc(item[2], item[3])
            return body
        block.tensor(mk_body("pe"))
        block.scalar(mk_body("act"))
        block.vector(mk_body("dve"))
        block.sync(mk_body("sp"))
        block.gpsimd(mk_body("pool"))


def sblock(h):
    b = h // 2
    return (1, 0) if b == 7 else (0, 65 * b)


def build_program(T):
    nCH = T // CK
    nQ = T // QW
    nG = nCH // 4
    TH = T // 2
    W4 = min(512, T)
    n4 = max(T // W4, 1)
    n512 = T // 512

    nc = bass.Bass("TRN2", target_bir_lowering=False)
    x_d = nc.dram_tensor("x", [D, T], F32, kind="ExternalInput")
    wqt_d = nc.dram_tensor("wqt", [D, D], F32, kind="ExternalInput")
    wkt_d = nc.dram_tensor("wkt", [D, D], F32, kind="ExternalInput")
    wvt_d = nc.dram_tensor("wvt", [D, D], F32, kind="ExternalInput")
    wot_d = nc.dram_tensor("wot", [D, D], F32, kind="ExternalInput")
    bias_d = nc.dram_tensor("bias_pack", [128, 32], F32, kind="ExternalInput")
    bvrow_d = nc.dram_tensor("bv_row", [1, D], F32, kind="ExternalInput")
    wgt_d = nc.dram_tensor("wgt", [D, FH], I32, kind="ExternalInput")
    wut_d = nc.dram_tensor("wut", [D, FH], I32, kind="ExternalInput")
    wdt_d = nc.dram_tensor("wdt", [FH, D], I32, kind="ExternalInput")
    coef_d = nc.dram_tensor("coef", [128, 1], F32, kind="ExternalInput")
    out_d = nc.dram_tensor("out", [D, T], F32, kind="ExternalOutput")
    cc_in = nc.dram_tensor("cc_in", [128, 520], F32)
    h_dram = nc.dram_tensor("h_dram", [FH // 2, T], BF16)
    cc_out = nc.dram_tensor("cc_out", [256, 520], F32)

    with ExitStack() as ctx:
        sb = lambda n, s, d: ctx.enter_context(nc.sbuf_tensor(n, s, d))
        fw = FW(nc, ctx)

        ident = sb("ident", [128, 128], BF16)
        ident32 = sb("ident32", [128, 128], F32)
        ones_c = sb("ones_c", [128, 1], BF16)
        onesr = sb("onesr", [1, 128], BF16)
        c1k = sb("c1k", [1, 128], BF16)
        ones1f = sb("ones1f", [1, 128], F32)
        bias_sb = sb("bias_sb", [128, 32], F32)
        bvb = sb("bvb", [128, D], BF16)
        coef_sb = sb("coef_sb", [128, 1], F32)
        xn = sb("xn", [128, NF * T], BF16)
        wgsb = [sb(f"wgsb{b}", [128, D], BF16) for b in range(2)]
        wusb = [sb(f"wusb{b}", [128, D], BF16) for b in range(2)]
        S_run = sb("S_run", [128, 520], F32)
        epsc = sb("epsc", [1, 1], F32)
        S_runb = sb("S_runb", [128, 520], BF16)

        block = ctx.enter_context(nc.Block())

        cBVR = ExitStack()
        bvr = cBVR.enter_context(nc.sbuf_tensor("bvr", [1, D], F32))
        cPSBV = ExitStack()
        ps_bv = cPSBV.enter_context(nc.psum_tensor("ps_bv", [128, 512], F32))

        e_bias = fw.dma("sp", lambda s: s.dma_start(out=bias_sb[:, :], in_=bias_d[:, :]), sem="ld_bias")
        e_coef = fw.dma("sp", lambda s: s.dma_start(out=coef_sb[:, :], in_=coef_d[:, :]), sem="ld_coef")
        e_ic0 = fw.emit("pool", lambda g: g.memset(ident[:, :], 1.0), inc="pool")
        e_id = fw.emit("pool", lambda g: g.affine_select(
            ident[:, :], ident[:, :], [[1, 128]], ALU.is_equal, 0.0,
            base=0, channel_multiplier=-1), wait=e_ic0, inc="pool")
        e_i320 = fw.emit("pool", lambda g: g.memset(ident32[:, :], 1.0), inc="pool")
        e_id32 = fw.emit("pool", lambda g: g.affine_select(
            ident32[:, :], ident32[:, :], [[1, 128]], ALU.is_equal, 0.0,
            base=0, channel_multiplier=-1), wait=e_i320, inc="pool")
        e_oc = fw.emit("dve", lambda v: v.memset(ones_c[:, :], 1.0), inc="dve")
        e_eps = fw.emit("dve", lambda v: v.memset(epsc[:, :], EPS), inc="dve")
        e_or = fw.emit("dve", lambda v: v.memset(onesr[:, :], 1.0), inc="dve")
        e_c1k = fw.emit("dve", lambda v: v.memset(c1k[:, :], 1.0 / D), inc="dve")
        e_o1f = fw.emit("dve", lambda v: v.memset(ones1f[:, :], 1.0), inc="dve")

        e_bvr = fw.dma("sp", lambda s: s.dma_start(out=bvr[:, :], in_=bvrow_d[:, :]), sem="ld_bvr")
        e_bvbc = []
        for half in range(2):
            e_mm = fw.emit("pe", lambda t, half=half: t.matmul(
                ps_bv[:, 0:512], ones1f[:, :], bvr[:, 512 * half:512 * half + 512],
                start=True, stop=True),
                wait=[e_o1f, e_bvr] + e_bvbc[-1:], inc="pe")
            e_ev = fw.emit("dve", lambda v, half=half: v.tensor_copy(
                bvb[:, 512 * half:512 * half + 512], ps_bv[:, 0:512]),
                wait=e_mm, inc="dve")
            e_bvbc.append(e_ev)
        cPSBV.close()
        cBVR.close()


        cS2 = ExitStack()
        sb2 = lambda n, s, d: cS2.enter_context(nc.sbuf_tensor(n, s, d))
        kfm = sb2("kfm", [128, NF * T], BF16)
        vtm = sb2("vtm", [128, nCH * 1040], BF16)
        qfm = sb2("qfm", [128, NF * T], BF16)
        wsb = [sb2(f"wsb{b}", [128, NF * D], BF16) for b in range(2)]
        ktm = [sb2(f"ktm{b}", [128, D], BF16) for b in range(2)]

        cS1 = ExitStack()
        sb1 = lambda n, s, d: cS1.enter_context(nc.sbuf_tensor(n, s, d))
        bfx = [sb1(f"bfx{b}", [128, NF * QW], BF16) for b in range(2)]
        sqr = [sb1(f"sqr{b}", [128, QW], BF16) for b in range(2)]
        rows = [sb1(f"rows{b}", [1, 4 * QW], F32) for b in range(2)]
        rowb = [sb1(f"rowb{b}", [1, 2 * QW], BF16) for b in range(2)]
        tmp = [sb1(f"tmp{b}", [128, QW], F32) for b in range(2)]
        phie = [sb1(f"phie{b}", [128, 512], BF16) for b in range(2)]
        phir = [sb1(f"phir{b}", [128, 512], BF16) for b in range(2)]
        S_in = sb1("S_in", [128, 520], F32)

        e_vone = fw.emit("dve", lambda v: v.memset(
            bass.AP(vtm, 64, [[nCH * 1040, 128], [1040, nCH], [65, H], [1, 1]]), 1.0),
            inc="dve")

        cPS2 = ExitStack()
        ps2f = lambda n, s, d: cPS2.enter_context(nc.psum_tensor(n, s, d))
        ps_st = ps2f("ps_st", [1, 512], F32)
        ps_mr = ps2f("ps_mr", [128, 512], F32)
        ps_p = [ps2f(f"ps_p{i}", [128, 512], F32) for i in range(2)]
        ps_S = ps2f("ps_S", [128, 512], F32)
        ps_S2 = ps2f("ps_S2", [128, 512], F32)
        ps_ktm = ps2f("ps_ktm", [128, 1024], BF16)

        def wload(dram, slot, semname, war=None):
            return fw.dma("pool", lambda g, slot=slot, dram=dram: g.dma_start(
                out=wsb[slot][:, :],
                in_=bass.AP(dram, 0, [[D, 128], [128 * D, NF], [1, D]])),
                wait=war, sem=semname)

        ld_ev = [None] * nQ
        ap_ev = [None] * nQ
        row_ev = [None] * nQ
        rowwar_ev = [None] * nQ
        bc_ev = [None] * nQ
        s2mm_ev = {}

        def emit_p1_quarter(qi):
            b = qi % 2
            cons = (ap_ev[qi - 2] + rowwar_ev[qi - 2]) if qi >= 2 else None
            ld_ev[qi] = fw.dma("pool", lambda g, b=b, qi=qi: g.dma_start(
                out=bfx[b][:, :],
                in_=bass.AP(x_d, QW * qi, [[T, 128], [128 * T, NF], [1, QW]])),
                wait=cons, sem=f"bfx{b}")
            e_s1 = None
            for j in range(NF):
                ws = [ld_ev[qi], e_oc] + e_bvbc[-1:]
                if qi >= 1:
                    ws += rowwar_ev[qi - 1]
                e_s1 = fw.emit("pe", lambda t, j=j, b=b: t.matmul(
                    ps_st[:, 0:QW], ones_c[:, :], bfx[b][:, QW * j:QW * (j + 1)],
                    start=(j == 0), stop=(j == NF - 1), skip_group_check=True),
                    wait=ws, inc="pe" if j == NF - 1 else None)
                sq_war = ([s2mm_ev[(qi, j - 2)]] if j >= 2 else
                          ([s2mm_ev[(qi - 1, NF - 2 + j)]] if qi >= 1 else []))
                e_sq = fw.emit("act", lambda sc, j=j, b=b: sc.activation(
                    out=sqr[j % 2][:, :], in_=bfx[b][:, QW * j:QW * (j + 1)],
                    func=AF.Square),
                    wait=[ld_ev[qi]] + sq_war, inc="act")
                s2mm_ev[(qi, j)] = fw.emit("pe", lambda t, j=j: t.matmul(
                    ps_st[:, QW:2 * QW], ones_c[:, :], sqr[j % 2][:, :],
                    start=False, stop=(j == NF - 1), skip_group_check=True),
                    wait=e_sq, inc="pe")
            e_s2 = s2mm_ev[(qi, NF - 1)]
            R = rows[b]
            e_r1 = fw.emit("dve", lambda v, R=R: v.tensor_copy(R[:, 0:QW], ps_st[:, 0:QW]),
                           wait=[e_s1, e_s2] + (rowwar_ev[qi - 2] if qi >= 2 else []), inc="dve")
            e_r2 = fw.emit("dve", lambda v, R=R: v.tensor_copy(R[:, QW:2 * QW], ps_st[:, QW:2 * QW]),
                           wait=[e_s2], inc="dve")
            e_a = fw.emit("dve", lambda v, R=R: v.tensor_mul(
                R[:, 2 * QW:3 * QW], R[:, 0:QW], R[:, 0:QW]), wait=[e_r1], inc="dve")
            e_b = fw.emit("dve", lambda v, R=R: v.scalar_tensor_tensor(
                R[:, 3 * QW:4 * QW], R[:, 2 * QW:3 * QW], -1.0 / D, R[:, QW:2 * QW],
                ALU.mult, ALU.add), wait=[e_a, e_r2], inc="dve")
            e_l = fw.emit("act", lambda sc, R=R: sc.activation(
                out=R[:, 2 * QW:3 * QW], in_=R[:, 3 * QW:4 * QW], func=AF.Ln,
                scale=1.0 / D, bias=epsc[0:1, 0:1]), wait=[e_b, e_eps], inc="act")
            e_rs = fw.emit("act", lambda sc, R=R: sc.activation(
                out=R[:, 3 * QW:4 * QW], in_=R[:, 2 * QW:3 * QW], func=AF.Exp,
                scale=-0.5), wait=e_l, inc="act")
            RB = rowb[b]
            e_m16 = fw.emit("dve", lambda v, R=R, RB=RB: v.tensor_copy(RB[:, 0:QW], R[:, 0:QW]),
                            wait=[e_r1] + ([bc_ev[qi - 2]] if qi >= 2 else []), inc="dve")
            e_rb16 = fw.emit("dve", lambda v, R=R, RB=RB: v.tensor_copy(
                RB[:, QW:2 * QW], R[:, 3 * QW:4 * QW]), wait=[e_rs], inc="dve")
            row_ev[qi] = e_rs
            rowwar_ev[qi] = [e_rs, e_m16, e_rb16]
            war = ap_ev[qi - 1] if qi >= 1 else []
            e_bm = fw.emit("pe", lambda t, RB=RB: t.matmul(
                ps_mr[:, 0:QW], c1k[:, :], RB[:, 0:QW],
                start=True, stop=True, skip_group_check=True),
                wait=[e_m16, e_c1k] + war, inc="pe")
            e_br = fw.emit("pe", lambda t, RB=RB: t.matmul(
                ps_mr[:, QW:2 * QW], onesr[:, :], RB[:, QW:2 * QW],
                start=False, stop=True, skip_group_check=True),
                wait=[e_rb16, e_or], inc="pe")
            evs = []
            for j in range(NF):
                e_sub = fw.emit("dve", lambda v, j=j, b=b: v.tensor_sub(
                    tmp[j % 2][:, :], bfx[b][:, QW * j:QW * (j + 1)], ps_mr[:, 0:QW]),
                    wait=[e_bm] + ([evs[j - 2]] if j >= 2 else []), inc="dve")
                e_mu = fw.emit("dve", lambda v, j=j, qi=qi: v.tensor_mul(
                    xn[:, j * T + QW * qi: j * T + QW * (qi + 1)],
                    tmp[j % 2][:, :], ps_mr[:, QW:2 * QW]),
                    wait=[e_sub, e_br], inc="dve")
                evs.append(e_mu)
            ap_ev[qi] = evs
            bc_ev[qi] = evs[-1]

        p_ev = [None, None]
        vch_ev = [None] * nCH
        vstate = {"last_vmm": None}
        kph_ev = {}
        trstate = {"ktm_war": [None, None], "ktm_bank": None, "last_Smm": None}

        def emit_v(i):
            ve = []
            for oh in range(2):
                sl = oh
                qi0 = (i * CK) // QW
                need = [ev for q in range(qi0, qi0 + max(CK // QW, 1)) for ev in ap_ev[q]]
                def vmm(t, i=i, oh=oh, sl=sl):
                    r = None
                    for kk in range(NF):
                        r = t.matmul(ps_p[sl][:, 0:512],
                                     xn[:, kk * T + CK * i: kk * T + CK * (i + 1)],
                                     wsb[0][:, D * kk + 512 * oh: D * kk + 512 * oh + 512],
                                     start=(kk == 0), stop=(kk == NF - 1))
                    return r
                vstate["last_vmm"] = fw.emit("pe", vmm,
                                             wait=[e_wv, p_ev[sl]] + need, inc="pe")
                e_evac = fw.emit("dve", lambda v, i=i, oh=oh, sl=sl: v.tensor_add(
                    bass.AP(vtm, i * 1040 + 65 * 8 * oh, [[nCH * 1040, 128], [65, 8], [1, 64]]),
                    ps_p[sl][:, 0:512], bvb[:, 512 * oh:512 * oh + 512]),
                    wait=[vstate["last_vmm"]] + e_bvbc[-1:], inc="dve")
                p_ev[sl] = e_evac
                ve.append(e_evac)
            vch_ev[i] = ve

        def emit_k(t4):
            for och in range(NF):
                sl = och % 2
                qi0 = (t4 * 512) // QW
                need = [ev for q in range(qi0, qi0 + 512 // QW) for ev in ap_ev[q]]
                def kmm(t, och=och, t4=t4, sl=sl):
                    r = None
                    for kk in range(NF):
                        r = t.matmul(ps_p[sl][:, 0:512],
                                     wsb[1][:, D * kk + 128 * och: D * kk + 128 * och + 128],
                                     xn[:, kk * T + 512 * t4: kk * T + 512 * (t4 + 1)],
                                     start=(kk == 0), stop=(kk == NF - 1))
                    return r
                e_mm = fw.emit("pe", kmm, wait=[e_wk, p_ev[sl]] + need, inc="pe")
                e_e = fw.emit("act", lambda sc, och=och, sl=sl: sc.activation(
                    out=phie[sl][:, :], in_=ps_p[sl][:, 0:512], func=AF.Exp,
                    bias=bias_sb[:, 8 + och:9 + och]),
                    wait=[e_mm, e_bias], inc="act")
                e_r = fw.emit("act", lambda sc, och=och, sl=sl: sc.activation(
                    out=phir[sl][:, :], in_=ps_p[sl][:, 0:512], func=AF.Relu,
                    bias=bias_sb[:, 8 + och:9 + och]),
                    wait=[e_mm], inc="act")
                e_phi = fw.emit("dve", lambda v, och=och, t4=t4, sl=sl: v.scalar_tensor_tensor(
                    kfm[:, och * T + 512 * t4: och * T + 512 * (t4 + 1)],
                    phie[sl][:, :], 1.0, phir[sl][:, :], ALU.min, ALU.add),
                    wait=[e_e, e_r], inc="dve")
                p_ev[sl] = e_phi
                kph_ev[(och, t4)] = e_phi

        def emit_trS(i):
            t4 = (i * CK) // 512
            kneed = [kph_ev[(och, t4)] for och in range(NF)]
            def ktr(t, i=i):
                r = None
                for och in range(NF):
                    r = t.transpose(ps_ktm[:, 128 * och:128 * (och + 1)],
                                    kfm[:, och * T + CK * i: och * T + CK * (i + 1)],
                                    ident[:, :])
                return r
            e_tr = fw.emit("pe", ktr,
                           wait=kneed + [e_id, trstate["ktm_war"][i % 2], trstate["ktm_bank"]],
                           inc="pe")
            e_kev = fw.emit("dve", lambda v, i=i: v.tensor_copy(ktm[i % 2][:, :], ps_ktm[:, :]),
                            wait=e_tr, inc="dve")
            trstate["ktm_war"][i % 2] = e_kev
            trstate["ktm_bank"] = e_kev
            for h in range(H):
                blk, off = sblock(h)
                pst = ps_S2 if blk else ps_S
                pb = (h % 2) * 64
                first = (i == 0) and (h in (0, 1, 14, 15))
                trstate["last_Smm"] = fw.emit("pe", lambda t, i=i, h=h, pst=pst, off=off, pb=pb, first=first: t.matmul(
                    pst[pb:pb + 64, off:off + 65],
                    ktm[i % 2][:, DH * h:DH * (h + 1)],
                    bass.AP(vtm, i * 1040 + 65 * h, [[nCH * 1040, 128], [1, 65]]),
                    start=first, stop=True, skip_group_check=True,
                    tile_position=(0, pb)),
                    wait=[e_kev, vch_ev[i][h // 8], e_vone],
                    inc="pe" if (i == nCH - 1 and h == H - 1) else None)

        emit_p1_quarter(0)
        emit_p1_quarter(1)
        e_wv = wload(wvt_d, 0, "ld_wv")
        e_wk = wload(wkt_d, 1, "ld_wk")
        for qp in range(nQ // 2):
            if qp + 1 < nQ // 2:
                emit_p1_quarter(2 * qp + 2)
                emit_p1_quarter(2 * qp + 3)
            for i in range(4 * qp, min(4 * qp + 4, nCH)):
                emit_v(i)
            emit_k(qp)
            for i in range(4 * qp, min(4 * qp + 4, nCH)):
                emit_trS(i)
        p1_psum_done = ap_ev[nQ - 1] + rowwar_ev[nQ - 1] + \
            (rowwar_ev[nQ - 2] if nQ >= 2 else []) + e_bvbc[-1:]
        last_vmm = vstate["last_vmm"]
        ktm_war = trstate["ktm_war"]
        last_Smm = trstate["last_Smm"]

        e_wq = wload(wqt_d, 0, "ld_wq", war=[last_vmm])
        e_sl1 = fw.emit("dve", lambda v: v.tensor_copy(S_run[:, 0:455], ps_S[:, 0:455]),
                        wait=last_Smm, inc="dve")
        e_sl2 = fw.emit("dve", lambda v: v.tensor_copy(S_run[:, 455:520], ps_S2[:, 0:65]),
                        wait=last_Smm, inc="dve")
        e_ccs = fw.dma("sp", lambda s: s.dma_start(out=cc_in[:, :], in_=S_run[:, :]),
                       wait=[e_sl1, e_sl2], sem="st_cc")
        e_cc = fw.emit("pool", lambda g: g.collective_compute(
            "AllGather", ALU.bypass,
            replica_groups=[[0, 1], [2, 3], [4, 5], [6, 7]],
            ins=[cc_in.ap().opt()], outs=[cc_out.ap().opt()]),
            wait=e_ccs, inc="cc")

        for och in range(NF):
            for t4 in range(n512):
                sl = (och * n512 + t4) % 2
                def qmm(t, och=och, t4=t4, sl=sl):
                    r = None
                    for kk in range(NF):
                        r = t.matmul(ps_p[sl][:, 0:512],
                                     wsb[0][:, D * kk + 128 * och: D * kk + 128 * och + 128],
                                     xn[:, kk * T + 512 * t4: kk * T + 512 * (t4 + 1)],
                                     start=(kk == 0), stop=(kk == NF - 1))
                    return r
                e_mm = fw.emit("pe", qmm, wait=[e_wq, p_ev[sl]], inc="pe")
                e_e = fw.emit("act", lambda sc, och=och, sl=sl: sc.activation(
                    out=phie[sl][:, :], in_=ps_p[sl][:, 0:512], func=AF.Exp,
                    bias=bias_sb[:, och:och + 1]),
                    wait=[e_mm], inc="act")
                e_r = fw.emit("act", lambda sc, och=och, sl=sl: sc.activation(
                    out=phir[sl][:, :], in_=ps_p[sl][:, 0:512], func=AF.Relu,
                    bias=bias_sb[:, och:och + 1]),
                    wait=[e_mm], inc="act")
                e_phi = fw.emit("dve", lambda v, och=och, t4=t4, sl=sl: v.scalar_tensor_tensor(
                    qfm[:, och * T + 512 * t4: och * T + 512 * (t4 + 1)],
                    phie[sl][:, :], 1.0, phir[sl][:, :], ALU.min, ALU.add),
                    wait=[e_e, e_r], inc="dve")
                p_ev[sl] = e_phi
                kph_ev[("q", och, t4)] = e_phi
        e_wo = wload(wot_d, 1, "ld_wo", war=[kph_ev[(NF - 1, n512 - 1)]])
        q_done = [kph_ev[("q", och, t4)] for och in range(NF) for t4 in range(n512)]
        e_sin = fw.dma("sp", lambda s: s.dma_start(out=S_in[:, :], in_=cc_out[0:128, :]),
                       wait=e_cc, sem="ld_sin")
        e_sr = fw.emit("dve", lambda v: v.tensor_single_scalar(
            S_run[:, :], S_in[:, :], coef_sb[:, 0:1], ALU.mult),
            wait=[e_sin, e_coef, e_ccs], inc="dve")
        e_srb = fw.emit("dve", lambda v: v.tensor_copy(S_runb[:, :], S_run[:, :]),
                        wait=e_sr, inc="dve")
        wg_ev = [None, None]
        wu_ev = [None, None]
        for j in range(2):
            wg_ev[j] = fw.dma("pool", lambda g, j=j: g.dma_start(
                out=wgsb[j][:, :],
                in_=bass.AP(wgt_d, 128 * j, [[FH, 128], [128 * FH, NF], [1, 128]])),
                sem=f"ld_wg{j}")
            wu_ev[j] = fw.dma("pool", lambda g, j=j: g.dma_start(
                out=wusb[j][:, :],
                in_=bass.AP(wut_d, 128 * j, [[FH, 128], [128 * FH, NF], [1, 128]])),
                sem=f"ld_wu{j}")
        cPS2.close()
        cS1.close()

        cS4 = ExitStack()
        sb4 = lambda n, s, d: cS4.enter_context(nc.sbuf_tensor(n, s, d))
        cPS4 = ExitStack()
        ps4f = lambda n, s, d: cPS4.enter_context(nc.psum_tensor(n, s, d))
        sTs = sb4("sTs", [128, 512], BF16)
        mask2 = sb4("mask2", [128, 256], BF16)
        A_tm = [sb4(f"A_tm{b}", [128, D], F32) for b in range(2)]
        A_fm = sb4("A_fm", [128, NF * 512], BF16)
        rcp = sb4("rcp", [128, 4], F32)
        ost = [sb4(f"ost{b}", [128, 512], F32) for b in range(2)]
        ps_sT = ps4f("ps_sT", [128, 1024], F32)
        ps_nm = ps4f("ps_nm", [128, 1024], F32)
        ps_fg = ps4f("ps_fg", [128, 512], F32)
        ps_fu = ps4f("ps_fu", [128, 512], F32)
        ps_kt4 = ps4f("ps_kt4", [128, 1024], BF16)
        ps_w = ps4f("ps_w", [128, 512], F32)
        WH = min(512, T)
        hbuf = [sb4(f"hbuf{b}", [128, WH], BF16) for b in range(2)]
        sgf = sb4("sgf", [128, WH], BF16)

        e_mk0 = fw.emit("pool", lambda g: g.memset(mask2[:, :], 1.0), inc="pool")
        e_mk = fw.emit("pool", lambda g: g.affine_select(
            mask2[:, :], mask2[:, :], [[0, 2], [1, 128]], ALU.is_ge, 0.0,
            base=0, channel_multiplier=-1), wait=e_mk0, inc="pool")

        scope_bar = [e_sl1, e_sl2, e_srb] + p1_psum_done + q_done + [p_ev[0], p_ev[1]]

        NJF = NJ // 2
        THF = T
        n4h = max(THF // 512, 1)

        class GUFiller:
            def __init__(self):
                self.seq = []
                for j in range(NJF):
                    for t4 in range(n4h):
                        self.seq.append((j, "g", t4))
                        self.seq.append((j, "u", t4))
                self.gi = 0
                self.kk = 0
                self.g_war = e_sl1
                self.u_war = e_sl2
                self.sg_war = None
                self.h_war = [None, None]
                self.gu_done = {}
                self.wev = {0: wg_ev[0], 1: wg_ev[1]}
                self.wuev = {0: wu_ev[0], 1: wu_ev[1]}
                self.hst = []
                self.last_mm = None
                self.first_bar = list(q_done) + p1_psum_done + [e_srb]
                self.nsg = 0
                self.nh = 0

            def done(self):
                return self.gi >= len(self.seq)

            def step(self, n):
                for _ in range(n):
                    if self.done():
                        return
                    j, kind, t4 = self.seq[self.gi]
                    b = j % 2
                    kk = self.kk
                    ws = None
                    if kk == 0:
                        ws = [self.wev[b] if kind == "g" else self.wuev[b],
                              self.g_war if kind == "g" else self.u_war]
                    wv = wgsb[b] if kind == "g" else wusb[b]
                    ps = ps_fg if kind == "g" else ps_fu
                    ev = fw.emit("pe", lambda t, kk=kk, t4=t4, wv=wv, ps=ps: t.matmul(
                        ps[:, 0:WH], wv[:, 128 * kk:128 * (kk + 1)],
                        xn[:, kk * T + WH * t4: kk * T + WH * (t4 + 1)],
                        start=(kk == 0), stop=(kk == NF - 1)),
                        wait=ws, inc="pe" if kk == NF - 1 else None)
                    self.kk += 1
                    if self.kk == NF:
                        self.kk = 0
                        self.gi += 1
                        if kind == "g":
                            e_sg = fw.emit("act", lambda sc: sc.activation(
                                out=sgf[:, :], in_=ps_fg[:, 0:WH], func=AF.Silu,
                                scale=QSC),
                                wait=[ev, self.sg_war] +
                                     (self.first_bar if self.nsg == 0 else []),
                                inc="act")
                            self.nsg += 1
                            self.g_war = e_sg
                            self.pend_sg = e_sg
                        else:
                            hsl = (j * n4h + t4) % 2
                            e_h = fw.emit("dve", lambda v, hsl=hsl: v.tensor_mul(
                                hbuf[hsl][:, :], sgf[:, :], ps_fu[:, 0:WH]),
                                wait=[ev, self.pend_sg, self.h_war[hsl]] +
                                     (self.first_bar if self.nh < 2 else []), inc="dve")
                            self.nh += 1
                            self.u_war = e_h
                            self.sg_war = e_h
                            self.gu_done[j] = e_h
                            e_st = fw.dma("sp", lambda s, j=j, t4=t4, hsl=hsl: s.dma_start(
                                out=h_dram[128 * j:128 * (j + 1), WH * t4:WH * (t4 + 1)],
                                in_=hbuf[hsl][:, :]),
                                wait=e_h, sem=f"hst{hsl}")
                            self.h_war[hsl] = e_st
                            self.hst.append(e_st)
                            if t4 == n4h - 1 and j + 2 < NJF:
                                jn = j + 2
                                self.wev[jn % 2] = fw.dma("pool", lambda g, jn=jn: g.dma_start(
                                    out=wgsb[jn % 2][:, :],
                                    in_=bass.AP(wgt_d, 128 * jn, [[FH, 128], [128 * FH, NF], [1, 128]])),
                                    wait=[self.gu_done[jn - 2]], sem=f"ld_wg{jn % 2}")
                                self.wuev[jn % 2] = fw.dma("pool", lambda g, jn=jn: g.dma_start(
                                    out=wusb[jn % 2][:, :],
                                    in_=bass.AP(wut_d, 128 * jn, [[FH, 128], [128 * FH, NF], [1, 128]])),
                                    sem=f"ld_wu{jn % 2}")

        guf = GUFiller()
        guf.step(96)
        sd_red = None
        sd_wr = None
        kt4_ev = None
        psw_ev = None
        wo_nm_ev = None
        afm_ev = {}
        msk_ev = {}
        rc_ev = {}
        dv_ev = {}
        m3e = {}
        ost_war = [None, None]
        upd_evs = [e_srb]
        for i in range(nCH):
            gslot = i % 4
            if i > 0:
                e_u1 = fw.emit("dve", lambda v: v.tensor_add(
                    S_run[:, 0:455], S_run[:, 0:455], ps_sT[:, 0:455]),
                    wait=[sd_wr], inc="dve")
                e_u2 = fw.emit("dve", lambda v: v.tensor_add(
                    S_run[:, 455:520], S_run[:, 455:520], ps_sT[:, 512:577]),
                    wait=[sd_wr], inc="dve")
                e_ub = fw.emit("dve", lambda v: v.tensor_copy(S_runb[:, :], S_run[:, :]),
                               wait=[e_u1, e_u2], inc="dve")
                sd_red = e_ub
                upd_evs = [e_ub]
            def ktr4(t, i=i):
                r = None
                for och in range(NF):
                    r = t.transpose(ps_kt4[:, 128 * och:128 * (och + 1)],
                                    kfm[:, och * T + CK * i: och * T + CK * (i + 1)],
                                    ident[:, :])
                return r
            e_tr4 = fw.emit("pe", ktr4,
                            wait=scope_bar + [ktm_war[i % 2], kt4_ev], inc="pe")
            e_k4 = fw.emit("dve", lambda v, i=i: v.tensor_copy(ktm[i % 2][:, :], ps_kt4[:, :]),
                           wait=e_tr4, inc="dve")
            kt4_ev = e_k4
            ktm_war[i % 2] = e_k4
            for p in range(H // 2):
                gp = i * (H // 2) + p
                m1e = None
                for hh in range(2):
                    h = 2 * p + hh
                    pb = (h % 2) * 64
                    m1e = fw.emit("pe", lambda t, i=i, h=h, pb=pb, hh=hh: t.matmul(
                        ps_sT[:, 512 * hh:512 * hh + 128],
                        kfm[pb:pb + 64, (h // 2) * T + CK * i: (h // 2) * T + CK * (i + 1)],
                        qfm[pb:pb + 64, (h // 2) * T + CK * i: (h // 2) * T + CK * (i + 1)],
                        start=True, stop=True, skip_group_check=True),
                        wait=scope_bar + [msk_ev.get(gp - 1)] + (upd_evs if p == 0 else []),
                        inc="pe" if hh == 1 else None)
                guf.step(5)
                e_m = fw.emit("dve", lambda v, p=p, gp=gp: v.tensor_mul(
                    sTs[:, 256 * (gp % 2):256 * (gp % 2) + 256],
                    bass.AP(ps_sT, 0, [[1024, 128], [512, 2], [1, 128]]),
                    mask2[:, :]),
                    wait=[m1e, e_mk, m3e.get(gp - 2)], inc="dve")
                msk_ev[gp] = e_m
                for hh in range(2):
                    h = 2 * p + hh
                    pb = (h % 2) * 64
                    fw.emit("pe", lambda t, i=i, h=h, hh=hh, gp=gp: t.matmul(
                        ps_nm[:, 512 * hh:512 * hh + 65],
                        sTs[:, 256 * (gp % 2) + 128 * hh:256 * (gp % 2) + 128 * hh + 128],
                        bass.AP(vtm, i * 1040 + 65 * h, [[nCH * 1040, 128], [1, 65]]),
                        start=True, stop=False, skip_group_check=True),
                        wait=[e_m, rc_ev.get(gp - 1), dv_ev.get(2 * (gp - 1) + hh),
                              wo_nm_ev if hh == 0 else None])
                    ev3 = fw.emit("pe", lambda t, h=h, i=i, pb=pb, hh=hh: t.matmul(
                        ps_nm[:, 512 * hh:512 * hh + 65],
                        qfm[pb:pb + 64, (h // 2) * T + CK * i: (h // 2) * T + CK * (i + 1)],
                        S_runb[pb:pb + 64, 65 * (h // 2):65 * (h // 2) + 65],
                        start=False, stop=True, skip_group_check=True),
                        wait=upd_evs, inc="pe" if hh == 1 else None)
                m3e[gp] = ev3
                e_rc = fw.emit("dve", lambda v, gp=gp: v.reciprocal(
                    rcp[:, 2 * (gp % 2):2 * (gp % 2) + 2],
                    bass.AP(ps_nm, 64, [[1024, 128], [512, 2], [1, 1]])),
                    wait=[ev3, dv_ev.get(2 * (gp - 2) + 1)], inc="dve")
                rc_ev[gp] = e_rc
                for hh in range(2):
                    h = 2 * p + hh
                    dv_ev[2 * gp + hh] = fw.emit("act", lambda sc, i=i, h=h, hh=hh, gp=gp: sc.activation(
                        out=A_tm[i % 2][:, DH * h:DH * (h + 1)],
                        in_=ps_nm[:, 512 * hh:512 * hh + 64],
                        func=AF.Identity, scale=rcp[:, 2 * (gp % 2) + hh:2 * (gp % 2) + hh + 1]),
                        wait=[e_rc, afm_ev.get((i - 2, 1))], inc="act")
            last_msk = msk_ev[i * (H // 2) + (H // 2) - 1]
            for h in range(H):
                blk, off = sblock(h)
                off = 512 if blk else off
                pb = (h % 2) * 64
                e_sd = fw.emit("pe", lambda t, i=i, h=h, off=off, pb=pb: t.matmul(
                    ps_sT[pb:pb + 64, off:off + 65],
                    ktm[i % 2][:, DH * h:DH * (h + 1)],
                    bass.AP(vtm, i * 1040 + 65 * h, [[nCH * 1040, 128], [1, 65]]),
                    start=True, stop=True, skip_group_check=True,
                    tile_position=(0, pb)),
                    wait=[kt4_ev, sd_red, last_msk], inc="pe" if h == H - 1 else None)
                if h == H - 1:
                    sd_wr = e_sd
            guf.step(2)
            chunk_divs = [dv_ev[2 * (i * (H // 2) + (H // 2) - 1) + 1]]
            for w in range(2):
                def atr(t, i=i, w=w):
                    r = None
                    for o in range(4):
                        och = 4 * w + o
                        r = t.transpose(ps_w[:, 128 * o:128 * (o + 1)],
                                        A_tm[i % 2][:, 128 * och:128 * (och + 1)],
                                        ident32[:, :])
                    return r
                e_at = fw.emit("pe", atr,
                               wait=chunk_divs + [psw_ev, wo_nm_ev, e_id32], inc="pe")
                e_ae = fw.emit("dve", lambda v, i=i, w=w, gslot=gslot: v.tensor_copy(
                    bass.AP(A_fm, 512 * 4 * w + 128 * gslot,
                            [[NF * 512, 128], [512, 4], [1, 128]]),
                    ps_w[:, 0:512]),
                    wait=e_at, inc="dve")
                psw_ev = e_ae
                afm_ev[(i, w)] = e_ae
            if gslot == 3:
                g4 = i // 4
                aneed = [afm_ev[(4 * g4 + c, w)] for c in range(4) for w in range(2)]
                last_gp = i * (H // 2) + (H // 2) - 1
                for och in range(NF):
                    if och % 2 == 0:
                        wob = ps_w[:, 0:512]
                        war = [psw_ev]
                    else:
                        wob = ps_nm[:, 0:512]
                        war = [rc_ev[last_gp], dv_ev[2 * last_gp], wo_nm_ev]
                    def womm(t, och=och, wob=wob):
                        r = None
                        for kk in range(NF):
                            r = t.matmul(wob,
                                         wsb[1][:, D * kk + 128 * och: D * kk + 128 * och + 128],
                                         A_fm[:, kk * 512:(kk + 1) * 512],
                                         start=(kk == 0), stop=(kk == NF - 1),
                                         skip_group_check=True)
                        return r
                    e_wm = fw.emit("pe", womm, wait=aneed + [e_wo] + war, inc="pe")
                    e_oev = fw.emit("dve", lambda v, och=och, wob=wob: v.tensor_scalar(
                        ost[och % 2][:, :], wob, bias_sb[:, 24 + och:25 + och],
                        None, ALU.add),
                        wait=[e_wm, e_bias, ost_war[och % 2]], inc="dve")
                    if och % 2 == 0:
                        psw_ev = e_oev
                    else:
                        wo_nm_ev = e_oev
                    ost_war[och % 2] = fw.dma("sp", lambda s, och=och, g4=g4: s.dma_start(
                        out=out_d[128 * och:128 * (och + 1), 512 * g4:512 * (g4 + 1)],
                        in_=ost[och % 2][:, :]),
                        wait=e_oev, sem=f"st_attn{och % 2}")
        while not guf.done():
            guf.step(8)
        n_attn_st = NF * nG
        last_gp_all = nCH * (H // 2) - 1
        p4_psum = [wo_nm_ev, psw_ev, kt4_ev, upd_evs[-1], sd_wr,
                   msk_ev[last_gp_all], rc_ev[last_gp_all],
                   dv_ev[2 * last_gp_all], dv_ev[2 * last_gp_all + 1],
                   guf.g_war, guf.u_war]
        p4_done = list(p4_psum)
        cPS4.close()
        cS4.close()
        cS2.close()

        nT4 = T // 512 if T >= 512 else 1
        cS3 = ExitStack()
        sb3 = lambda n, s, d: cS3.enter_context(nc.sbuf_tensor(n, s, d))
        cPS3 = ExitStack()
        ps3f = lambda n, s, d: cPS3.enter_context(nc.psum_tensor(n, s, d))
        hsb = sb3("hsb", [128, NJF * T], BF16)
        hsb0 = sb3("hsb0", [128, NJF * T], BF16)
        wdsb = sb3("wdsb", [128, NJ * D], BF16)
        sgt = [sb3(f"sgt{b}", [128, W4], BF16) for b in range(2)]
        x3 = [sb3(f"x3{b}", [128, W4], F32) for b in range(2)]
        ost3 = [sb3(f"ost3{b}", [128, W4], F32) for b in range(2)]
        ps_g = [ps3f(f"ps_g{b}", [128, 512], F32) for b in range(2)]
        ps_u = [ps3f(f"ps_u{b}", [128, 512], F32) for b in range(2)]
        ps_d = [ps3f(f"ps_d{b}", [128, 512], F32) for b in range(2)]

        st_attn_total = [Ev("st_attn0", fw.semcnt["st_attn0"]),
                         Ev("st_attn1", fw.semcnt["st_attn1"]),
                         Ev("hst0", fw.semcnt["hst0"]),
                         Ev("hst1", fw.semcnt["hst1"])]
        p4_done = p4_done + st_attn_total
        wg2 = [None, None]
        wu2 = [None, None]
        for j in range(NJF, NJF + 2):
            wg2[j % 2] = fw.dma("pool", lambda g, j=j: g.dma_start(
                out=wgsb[j % 2][:, :],
                in_=bass.AP(wgt_d, 128 * j, [[FH, 128], [128 * FH, NF], [1, 128]])),
                wait=[guf.gu_done[NJF - 2], guf.gu_done[NJF - 1]], sem=f"ld_wg{j % 2}")
            wu2[j % 2] = fw.dma("pool", lambda g, j=j: g.dma_start(
                out=wusb[j % 2][:, :],
                in_=bass.AP(wut_d, 128 * j, [[FH, 128], [128 * FH, NF], [1, 128]])),
                sem=f"ld_wu{j % 2}")
        wd_parts = []
        h_parts = []
        e_acc_last = None
        g_ev = [None, None]
        u_ev = [None, None]
        d_ev = [None, None]
        o3_ev = [None, None]
        sg_war = [None, None]
        gu_n = 0
        hj_ev = {}

        for j in range(NJF, NJ):
            b = j % 2
            if j >= NJF + 2:
                war = [hj_ev[j - 2]]
                wg2[b] = fw.dma("pool", lambda g, j=j, b=b: g.dma_start(
                    out=wgsb[b][:, :],
                    in_=bass.AP(wgt_d, 128 * j, [[FH, 128], [128 * FH, NF], [1, 128]])),
                    wait=war, sem=f"ld_wg{b}")
                wu2[b] = fw.dma("pool", lambda g, j=j, b=b: g.dma_start(
                    out=wusb[b][:, :],
                    in_=bass.AP(wut_d, 128 * j, [[FH, 128], [128 * FH, NF], [1, 128]])),
                    sem=f"ld_wu{b}")
            for t4 in range(n4):
                col = W4 * t4
                sgsl = gu_n % 2
                gu_n += 1
                def gmm(t, j=j, b=b, col=col):
                    r = None
                    for kk in range(NF):
                        r = t.matmul(ps_g[b][:, 0:W4], wgsb[b][:, 128 * kk:128 * (kk + 1)],
                                     xn[:, kk * T + col: kk * T + col + W4],
                                     start=(kk == 0), stop=(kk == NF - 1))
                    return r
                e_gm = fw.emit("pe", gmm,
                               wait=[wg2[b], g_ev[b]] + (p4_psum if j < NJF + 2 else []),
                               inc="pe")
                def umm(t, j=j, b=b, col=col):
                    r = None
                    for kk in range(NF):
                        r = t.matmul(ps_u[b][:, 0:W4], wusb[b][:, 128 * kk:128 * (kk + 1)],
                                     xn[:, kk * T + col: kk * T + col + W4],
                                     start=(kk == 0), stop=(kk == NF - 1))
                    return r
                e_um = fw.emit("pe", umm, wait=[wu2[b], u_ev[b]], inc="pe")
                e_sg = fw.emit("act", lambda sc, b=b, sgsl=sgsl: sc.activation(
                    out=sgt[sgsl][:, :], in_=ps_g[b][:, 0:W4], func=AF.Silu,
                    scale=QSC),
                    wait=[e_gm, g_ev[b], sg_war[sgsl]] + (p4_done if gu_n <= 2 else []),
                    inc="act")
                g_ev[b] = e_sg
                e_h = fw.emit("dve", lambda v, j=j, col=col, b=b, sgsl=sgsl: v.tensor_mul(
                    hsb[:, (j - NJF) * T + col: (j - NJF) * T + col + W4],
                    sgt[sgsl][:, :], ps_u[b][:, 0:W4]),
                    wait=[e_sg, e_um] + (p4_done if j < NJF + 2 else []), inc="dve")
                sg_war[sgsl] = e_h
                u_ev[b] = e_h
                hj_ev[j] = e_h
            if NJF + 1 <= j <= NJF + 8:
                p = j - NJF - 1
                wd_parts.append(fw.dma("pool", lambda g, p=p: g.dma_start(
                    out=wdsb[:, D * 3 * p:D * 3 * (p + 1)],
                    in_=bass.AP(wdt_d, 128 * 1024 * 3 * p,
                                [[1024, 128], [128 * 1024, 3], [1, 1024]])),
                    wait=p4_done if p == 0 else None,
                    sem="ld_wdA" if p < 4 else "ld_wdB"))
            if NJF + 2 <= j <= NJF + 7:
                p = j - NJF - 2
                h_parts.append(fw.dma("sp", lambda s, p=p: s.dma_start(
                    out=hsb0[:, 2 * T * p:2 * T * (p + 1)],
                    in_=bass.AP(h_dram, 128 * T * 2 * p,
                                [[T, 128], [128 * T, 2], [1, T]])),
                    wait=p4_done if p == 0 else None, sem="ld_h"))
        e_wdA = wd_parts[3]
        e_wdB = wd_parts[7]
        e_hload = h_parts[-1]
        hdone = [hj_ev[j] for j in range(NJF, NJ)]
        for t4 in range(n4):
            for kk in range(NF):
                sl = kk % 2
                col = W4 * t4
                e_x3 = fw.dma("sp", lambda s, kk=kk, sl=sl, col=col: s.dma_start(
                    out=x3[sl][:, :],
                    in_=x_d[128 * kk:128 * (kk + 1), col:col + W4]),
                    wait=[o3_ev[sl]] if o3_ev[sl] else p4_done, sem=f"x3_{sl}")
                def dsegA(t, kk=kk, sl=sl, col=col):
                    r = None
                    for j in range(NJF):
                        r = t.matmul(ps_d[sl][:, 0:W4],
                                     wdsb[:, D * j + 128 * kk: D * j + 128 * kk + 128],
                                     hsb0[:, j * T + col: j * T + col + W4],
                                     start=(j == 0), stop=False)
                    return r
                fw.emit("pe", dsegA, wait=[e_wdA, e_hload, d_ev[sl]])
                def dsegB(t, kk=kk, sl=sl, col=col):
                    r = None
                    for j in range(NJF, NJ):
                        r = t.matmul(ps_d[sl][:, 0:W4],
                                     wdsb[:, D * j + 128 * kk: D * j + 128 * kk + 128],
                                     hsb[:, (j - NJF) * T + col: (j - NJF) * T + col + W4],
                                     start=False, stop=(j == NJ - 1))
                    return r
                e_dm = fw.emit("pe", dsegB, wait=[e_wdB] + hdone, inc="pe")
                e_o3 = fw.emit("dve", lambda v, kk=kk, sl=sl: v.scalar_tensor_tensor(
                    ost3[sl][:, :], ps_d[sl][:, 0:W4], QSC * QSC, x3[sl][:, :],
                    ALU.mult, ALU.add),
                    wait=[e_dm, e_x3, o3_ev[sl]], inc="dve")
                d_ev[sl] = e_o3
                e_st = fw.dma("pool", lambda g, kk=kk, sl=sl, col=col: g.dma_start(
                    out=out_d[128 * kk:128 * (kk + 1), col:col + W4],
                    in_=ost3[sl][:, :], accum_op=ALU.add),
                    wait=[e_o3] + st_attn_total, sem=f"st_out{sl}")
                o3_ev[sl] = e_st
                e_acc_last = e_st
        cPS3.close()
        cS3.close()

        out_final = [Ev("st_out0", fw.semcnt["st_out0"]), Ev("st_out1", fw.semcnt["st_out1"])]
        fw.emit("pool", lambda g: g.nop(), wait=out_final)
        fw.emit("sp", lambda s: s.nop(), wait=st_attn_total + out_final)
        fw.finish(block)
    return nc


_NC_CACHE = {}


def _get_nc(T):
    if T not in _NC_CACHE:
        _NC_CACHE[T] = build_program(T)
    return _NC_CACHE[T]


def kernel(L0_attn_x, L0_attn_Wq, L0_attn_bq, L0_attn_Wk, L0_attn_bk,
           L0_attn_Wv, L0_attn_bv, L0_attn_Wo, L0_attn_bo,
           L0_ffn_w_gate, L0_ffn_w_up, L0_ffn_w_down):
    x = np.asarray(L0_attn_x)
    B, Sq, _ = x.shape
    T = (B * Sq) // 8
    xT = np.ascontiguousarray(x.transpose(0, 2, 1))

    def wT(w):
        return np.ascontiguousarray(np.asarray(w).T)

    wqt, wkt, wvt, wot = wT(L0_attn_Wq), wT(L0_attn_Wk), wT(L0_attn_Wv), wT(L0_attn_Wo)
    wgt = np.ascontiguousarray(np.asarray(L0_ffn_w_gate).T)
    wut = np.ascontiguousarray(np.asarray(L0_ffn_w_up).T)
    wdt = np.ascontiguousarray(np.asarray(L0_ffn_w_down).T)

    def pack(bv):
        return np.ascontiguousarray(np.asarray(bv, np.float32).reshape(8, 128).T)

    bias_pack = np.ascontiguousarray(np.concatenate(
        [pack(L0_attn_bq), pack(L0_attn_bk), pack(L0_attn_bv), pack(L0_attn_bo)], axis=1))
    bv_row = np.ascontiguousarray(np.asarray(L0_attn_bv, np.float32)[None, :])

    in_maps = []
    for c in range(8):
        b, g = c // 2, c % 2
        in_maps.append({
            "x": np.ascontiguousarray(xT[b][:, g * T:(g + 1) * T]),
            "wqt": wqt, "wkt": wkt, "wvt": wvt, "wot": wot,
            "bias_pack": bias_pack, "bv_row": bv_row,
            "wgt": wgt, "wut": wut, "wdt": wdt,
            "coef": np.full((128, 1), float(g), np.float32),
        })
    nc = _get_nc(T)
    res = run_bass_kernel_spmd(nc, in_maps, core_ids=list(range(8)))
    full = np.empty((B, D, Sq), np.float32)
    for c in range(8):
        b, g = c // 2, c % 2
        full[b][:, g * T:(g + 1) * T] = res.results[c]["out"]
    return np.ascontiguousarray(full.transpose(0, 2, 1))
